# version 19
# baseline (speedup 1.0000x reference)
import numpy as np
import ml_dtypes
from contextlib import ExitStack

import concourse.bass as bass
import concourse.mybir as mybir
from concourse.bass_utils import run_bass_kernel_spmd

F32 = mybir.dt.float32
BF16 = mybir.dt.bfloat16
U32 = mybir.dt.uint32
AF = mybir.ActivationFunctionType
ALU = mybir.AluOpType
AX = mybir.AxisListType
f32 = np.float32

ENGINES = ("tensor", "vector", "scalar", "gpsimd", "sync")

S, D, NT = 4096, 1024, 32
HA = 3072
FF = 4096
NEXP, RK, TOPK = 256, 64, 32
EPS = 1e-5


class RV:
    __slots__ = ("h", "reg")

    def __init__(self, h, reg):
        self.h = h
        self.reg = reg

    def __getitem__(self, k):
        return RV(self.h[k], self.reg)

    @property
    def ap(self):
        return self.h


class Builder:
    def __init__(self, nc):
        self.nc = nc
        self.ops = []
        self.rw = {}
        self.rr = {}

    def op(self, eng, fn, reads=(), writes=(), dma=False):
        idx = len(self.ops)
        node = ("dma", idx) if dma else ("op", idx)
        deps = set()
        for r in reads:
            w = self.rw.get(r)
            if w is not None:
                deps.add(w)
        for r in writes:
            w = self.rw.get(r)
            if w is not None:
                deps.add(w)
            for rd in self.rr.get(r, ()):
                deps.add(rd)
        deps.discard(node)
        self.ops.append(dict(idx=idx, eng=eng, fn=fn, dma=dma, deps=deps))
        for r in reads:
            self.rr.setdefault(r, []).append(node)
        for r in writes:
            self.rw[r] = node
            self.rr[r] = []
        return idx

    def barrier(self):
        last = {}
        for i, o in enumerate(self.ops):
            if o["fn"] is None and not o["dma"]:
                continue
            key = (o["eng"], o["dma"])
            last[key] = ("dma", i) if o["dma"] else ("op", i)
        nodes = set(last.values())
        for e in ENGINES:
            deps = set()
            for n in nodes:
                if n[0] == "op" and self.ops[n[1]]["eng"] == e:
                    continue
                deps.add(n)
            self.ops.append(dict(idx=len(self.ops), eng=e, fn=None, dma=False, deps=deps))
        self.rw.clear()
        self.rr.clear()

    def finalize(self):
        need = set()
        for o in self.ops:
            for n in o["deps"]:
                if n[0] == "op":
                    if self.ops[n[1]]["eng"] == o["eng"] == "tensor":
                        continue
                    need.add(n[1])
        inc_no, dma_no = {}, {}
        cnt = {e: 0 for e in ENGINES}
        dcnt = {e: 0 for e in ENGINES}
        for i, o in enumerate(self.ops):
            if o["dma"]:
                dcnt[o["eng"]] += 1
                dma_no[i] = dcnt[o["eng"]] * 16
            elif i in need:
                cnt[o["eng"]] += 1
                inc_no[i] = cnt[o["eng"]]
        VC = {e: {} for e in ENGINES}
        lastinc = {e: 0 for e in ENGINES}
        snap = {}
        for i, o in enumerate(self.ops):
            e = o["eng"]
            waits = []
            for n in sorted(o["deps"]):
                j = n[1]
                te = self.ops[j]["eng"]
                if n[0] == "op":
                    if te == e == "tensor":
                        continue
                    key, tc = te, inc_no[j]
                else:
                    key, tc = te + "*", dma_no[j]
                if VC[e].get(key, 0) >= tc:
                    continue
                waits.append((key, tc))
                VC[e][key] = tc
                for k2, v2 in snap.get(n, {}).items():
                    if VC[e].get(k2, 0) < v2:
                        VC[e][k2] = v2
            o["waits"] = waits
            if o["dma"]:
                o["selfwait"] = dma_no[i]
                VC[e][e + "*"] = dma_no[i]
                s = dict(VC[e])
                s[e] = max(s.get(e, 0), lastinc[e])
                snap[("dma", i)] = s
            elif i in inc_no:
                lastinc[e] = inc_no[i]
                s = dict(VC[e])
                s[e] = inc_no[i]
                snap[("op", i)] = s
        self.inc_no = inc_no

    def play(self, block, sems):
        inc_no = self.inc_no
        for ename in ENGINES:
            ops = [o for o in self.ops if o["eng"] == ename]

            def body(eng, _ops=ops, _e=ename):
                for o in _ops:
                    for key, val in o["waits"]:
                        eng.wait_ge(sems[key], val)
                    if o["fn"] is None:
                        continue
                    inst = o["fn"](eng)
                    if o["dma"]:
                        inst.then_inc(sems[_e + "*"], 16)
                        eng.wait_ge(sems[_e + "*"], o["selfwait"])
                    elif o["idx"] in inc_no:
                        inst.then_inc(sems[_e], 1)

            getattr(block, ename)(body)


class Pool:
    def __init__(self, handle, ncols, tag):
        self.h = handle
        self.n = ncols
        self.tag = tag
        self.o = 0
        self.cnt = 0

    def take(self, cols, name):
        o = (self.o + 15) & ~15
        assert o + cols <= self.n, f"pool {self.tag} overflow: {name} needs {cols} at {o}/{self.n}"
        self.o = o + cols
        self.cnt += 1
        return RV(self.h[:, o:o + cols], f"{self.tag}.{name}.{self.cnt}")

    def reset(self):
        self.o = 0


def build_program(dbg=False):
    nc = bass.Bass("TRN2", target_bir_lowering=False)
    b = Builder(nc)

    def din(name, shape, dt=F32):
        return RV(nc.dram_tensor(name, shape, dt, kind="ExternalInput").ap(), "d." + name)

    def dint(name, shape, dt=F32):
        if dbg:
            return RV(nc.dram_tensor(name, shape, dt, kind="ExternalOutput").ap(), "d." + name)
        return RV(nc.dram_tensor(name, shape, dt).ap(), "d." + name)

    x_d = din("x", [S, D])
    wv_d = din("wv", [D, D])
    wo_d = din("wo", [D, D])
    wfi_d = din("wfi", [D, 2])
    w1a_d = din("w1a", [D, HA])
    w2a_d = din("w2a", [HA, D])
    bv_d = din("bv", [1, D])
    bo_d = din("bo", [1, D])
    bfi_d = din("bfi", [1, 2])
    g1_d = din("g1", [1, D])
    b1_d = din("b1", [1, D])
    g2_d = din("g2", [1, D])
    b2_d = din("b2", [1, D])
    b1c_d = din("b1c", [128, 24])
    ln1gc_d = din("ln1gc", [128, 8])
    ln1bc_d = din("ln1bc", [128, 8])
    ln2gc_d = din("ln2gc", [128, 8])
    ln2bc_d = din("ln2bc", [128, 8])
    b2a_d = din("b2a", [1, D])
    a1t_d = din("a1t", [NEXP * RK, D])
    b1t_d = din("b1t", [NEXP * RK, FF])
    a2t_d = din("a2t", [NEXP * RK, FF])
    b2t_d = din("b2t", [NEXP * RK, D])
    l1r_d = din("l1r", [1, NEXP])
    l1c_d = din("l1c", [128, 2])
    l2r_d = din("l2r", [1, NEXP])
    l2c_d = din("l2c", [128, 2])
    tri_d = din("tri", [128, 128])
    onesq_d = din("onesq", [128, 128])
    ones1_d = din("ones1", [1, 128])
    idf_d = din("idf", [128, 128])
    maskm_d = din("maskm", [128, 128])
    maska_d = din("maska", [128, 128])
    iota32_d = din("iota32", [128, 32])
    kvals_d = din("kvals", [128, 2])
    r64_d = din("r64", [128, 1])
    esel_d = din("esel", [32, 2048])
    ocbf_d = din("ocbf", [128, 1], BF16)
    ones1b_d = din("ones1b", [1, 128], BF16)

    out_d = RV(nc.dram_tensor("out", [S, D], F32, kind="ExternalOutput").ap(), "d.out")
    xb_d = dint("xb_scr", [S, D])
    x1_d = dint("x1_scr", [S, D])
    w1c_d = dint("w1c_scr", [128, 8 * FF], BF16)
    w2c_d = dint("w2c_scr", [128, 32 * D], BF16)
    g_d = dint("g_scr", [128, 131072], BF16)
    m_d = dint("m_scr", [128, 98304], BF16)

    st = ExitStack()

    def sb(name, shape, dt=F32):
        t = st.enter_context(nc.sbuf_tensor(name, shape, dt))
        return RV(t[tuple(slice(None) for _ in shape)], name)

    c_tri = sb("c_tri", [128, 128])
    c_ones = sb("c_ones", [128, 128])
    c_ones1 = sb("c_ones1", [1, 128])
    c_idf = sb("c_idf", [128, 128])
    c_maskm = sb("c_maskm", [128, 128])
    c_maska = sb("c_maska", [128, 128])
    c_iota32 = sb("c_iota32", [128, 32])
    c_kvals = sb("c_kvals", [128, 2])
    c_r64 = sb("c_r64", [128, 1])
    c_ocbf = sb("c_ocbf", [128, 1], BF16)
    c_ones1b = sb("c_ones1b", [1, 128], BF16)
    c_l1r = sb("c_l1r", [1, NEXP])
    c_l1c = sb("c_l1c", [128, 2])
    c_l2r = sb("c_l2r", [1, NEXP])
    c_l2c = sb("c_l2c", [128, 2])

    ccols = sb("ccols", [128, NT])
    dbc3 = [sb(f"dbc{p}", [128, 128]) for p in range(3)]
    gc1 = sb("gc1", [128, 8])
    bc1 = sb("bc1", [128, 8])
    bfib = sb("bfib", [128, 2])
    offs = sb("offs", [128, 2])
    stt = sb("stt", [128, 96])
    astt = sb("astt", [128, 8])
    fiz = sb("fiz", [128, 48])
    powc = sb("powc", [128, 1])
    s1_all = sb("s1_all", [128, 16])
    s2_all = sb("s2_all", [128, 16])
    idx1 = sb("idx1", [128, 16], U32)
    idx2 = sb("idx2", [128, 16], U32)
    rows = sb("rows", [1, 512])

    PB16 = Pool(st.enter_context(nc.sbuf_tensor("PB16", [128, 50400], BF16)), 50400, "b")
    PF32 = Pool(st.enter_context(nc.sbuf_tensor("PF32", [128, 25408], F32)), 25408, "f")

    PS = [RV(st.enter_context(nc.psum_tensor(f"PS{i}", [128, 512], F32))[:, :], f"PS{i}")
          for i in range(8)]

    sems = {}
    for e in ENGINES:
        sems[e] = st.enter_context(nc.semaphore("sem_" + e))
    for e in ("sync", "gpsimd", "scalar"):
        sems[e + "*"] = st.enter_context(nc.semaphore("semd_" + e))

    def mm(out, lhsT, rhs, start=True, stop=True):
        b.op("tensor",
             lambda e, o=out.ap, l=lhsT.ap, r=rhs.ap, s=start, t=stop: e.matmul(o, l, r, start=s, stop=t),
             reads=[lhsT.reg, rhs.reg], writes=[out.reg])

    def tr(out, in_):
        b.op("tensor", lambda e, o=out.ap, i=in_.ap, d=c_idf.ap: e.transpose(o, i, d),
             reads=[in_.reg, "c_idf"], writes=[out.reg])

    def act(out, in_, func, bias=None, scale=None, accum=None):
        kw = {}
        rd = [in_.reg]
        if bias is not None:
            if isinstance(bias, RV):
                kw["bias"] = bias.ap
                rd.append(bias.reg)
            else:
                kw["bias"] = f32(bias)
        if scale is not None:
            if isinstance(scale, RV):
                kw["scale"] = scale.ap
                rd.append(scale.reg)
            else:
                kw["scale"] = f32(scale)
        wr = [out.reg]
        if accum is not None:
            kw["accum_out"] = accum.ap
            wr.append(accum.reg)
        b.op("scalar", lambda e, o=out.ap, i=in_.ap, f=func, k=kw: e.activation(o, i, f, **k),
             reads=rd, writes=wr)

    def ts(out, in0, s1, s2, op0, op1=None, eng="vector"):
        rd = [in0.reg]
        a1, a2 = s1, s2
        if isinstance(s1, RV):
            a1 = s1.ap
            rd.append(s1.reg)
        if isinstance(s2, RV):
            a2 = s2.ap
            rd.append(s2.reg)
        kw = {}
        if op1 is not None:
            kw["op1"] = op1
        b.op(eng, lambda e, o=out.ap, i=in0.ap, x=a1, y=a2, p0=op0, k=kw: e.tensor_scalar(o, i, x, y, op0=p0, **k),
             reads=rd, writes=[out.reg])

    def tt(out, in0, in1, op, eng="vector"):
        b.op(eng, lambda e, o=out.ap, i=in0.ap, j=in1.ap, p=op: e.tensor_tensor(o, i, j, op=p),
             reads=[in0.reg, in1.reg], writes=[out.reg])

    def red(out, in_, which="sum"):
        fn = "reduce_sum" if which == "sum" else "reduce_max"
        b.op("vector", lambda e, o=out.ap, i=in_.ap, f=fn: getattr(e, f)(o, i, axis=AX.X),
             reads=[in_.reg], writes=[out.reg])

    def cp(out, in_, eng="vector"):
        b.op(eng, lambda e, o=out.ap, i=in_.ap: e.tensor_copy(o, i),
             reads=[in_.reg], writes=[out.reg])

    def recip(out, in_):
        b.op("vector", lambda e, o=out.ap, i=in_.ap: e.reciprocal(o, i),
             reads=[in_.reg], writes=[out.reg])

    def mset(t, val, eng="vector"):
        b.op(eng, lambda e, o=t.ap, v=val: e.memset(o, v), reads=[], writes=[t.reg])

    def dma(q, out, in_):
        b.op(q, lambda e, o=out.ap, i=in_.ap: e.dma_start(o, i),
             reads=[in_.reg], writes=[out.reg], dma=True)

    def idma(out, table, idx, eloff):
        b.op("gpsimd",
             lambda e, o=out.ap, t=table.ap, ix=idx.ap, el=eloff:
             e.indirect_dma_start(o, None, t, bass.IndirectOffsetOnAxis(ix, 0), element_offset=el),
             reads=[table.reg, idx.reg], writes=[out.reg], dma=True)

    def load_row_and_bcast(row_dram, out, width):
        for n0 in range(0, width, 512):
            w = min(512, width - n0)
            dma("sync", rows[:, 0:w], row_dram[:, n0:n0 + w])
            mm(PS[7][:, 0:w], c_ones1, rows[:, 0:w])
            cp(out[:, n0:n0 + w], PS[7][:, 0:w])

    def ln_stats(xrv, c0, via_pow=False):
        b.op("vector", lambda e, o=stt.h[:, c0 + 4:c0 + 10], i=xrv[:, 0:512].ap: e.bn_stats(o, i),
             reads=[xrv.reg], writes=["stt"])
        b.op("vector", lambda e, o=stt.h[:, c0 + 10:c0 + 16], i=xrv[:, 512:1024].ap: e.bn_stats(o, i),
             reads=[xrv.reg], writes=["stt"])
        b.op("vector", lambda e, o=stt.h[:, c0:c0 + 2], i=stt.h[:, c0 + 4:c0 + 16]: e.bn_aggr(o, i),
             reads=["stt"], writes=["stt"])
        mean = stt[:, c0:c0 + 1]
        var = stt[:, c0 + 1:c0 + 2]
        rstd = stt[:, c0 + 2:c0 + 3]
        if via_pow:
            ts(stt[:, c0 + 3:c0 + 4], var, EPS, None, ALU.add)
            tt(rstd, stt[:, c0 + 3:c0 + 4], powc, ALU.pow, eng="gpsimd")
        else:
            act(stt[:, c0 + 3:c0 + 4], var, AF.Ln, bias=EPS)
            act(rstd, stt[:, c0 + 3:c0 + 4], AF.Exp, scale=-0.5)
        return mean, rstd

    def transpose_tile(xrv, dst_fn):
        for gph in range(2):
            ps = PS[5 + gph]
            for k in range(4):
                c = gph * 4 + k
                tr(ps[:, k * 128:(k + 1) * 128], xrv[:, c * 128:(c + 1) * 128])
            for k in range(4):
                c = gph * 4 + k
                cp(dst_fn(c), ps[:, k * 128:(k + 1) * 128])

    def transpose_affine(xrv, dst_fn, gc, bc):
        for gph in range(2):
            ps = PS[5 + gph]
            for k in range(4):
                c = gph * 4 + k
                tr(ps[:, k * 128:(k + 1) * 128], xrv[:, c * 128:(c + 1) * 128])
            for k in range(4):
                c = gph * 4 + k
                act(dst_fn(c), ps[:, k * 128:(k + 1) * 128], AF.Identity,
                    scale=gc[:, c:c + 1], bias=bc[:, c:c + 1])

    dma("sync", c_tri, tri_d)
    dma("sync", c_ones, onesq_d)
    dma("sync", c_ones1, ones1_d)
    dma("sync", c_idf, idf_d)
    dma("sync", c_maskm, maskm_d)
    dma("sync", c_maska, maska_d)
    dma("sync", c_iota32, iota32_d)
    dma("sync", c_kvals, kvals_d)
    dma("sync", c_r64, r64_d)
    dma("sync", c_ocbf, ocbf_d)
    dma("sync", c_ones1b, ones1b_d)
    dma("sync", c_l1r, l1r_d)
    dma("sync", c_l1c, l1c_d)
    dma("sync", c_l2r, l2r_d)
    dma("sync", c_l2c, l2c_d)
    mset(powc, -0.5, eng="gpsimd")

    tk = PF32.take(656, "tk")
    idxf = PF32.take(16, "idxf")
    idxg = PF32.take(16, "idxg")
    esel_p = PF32.take(2048, "esel")
    c_esel = RV(esel_p.h[0:32, :], esel_p.reg)
    dma("sync", c_esel, esel_d)

    def topk(lr, lc, s_all, idxu):
        lrb = tk[:, 0:256]
        gscr = tk[:, 256:512]
        rk_ = tk[:, 512:514]
        gm = tk[:, 514:515]
        nm = tk[:, 515:516]
        em = tk[:, 516:518]
        mk = tk[:, 518:520]
        emm = tk[:, 520:522]
        tot = tk[:, 522:524]
        ssum = tk[:, 524:525]
        sinv = tk[:, 525:526]
        wf = tk[:, 526:528]
        wslot = RV(tk.h[0:32, 528:529], tk.reg)
        kslot = RV(tk.h[0:32, 529:530], tk.reg)
        mm(PS[7][:, 0:256], c_ones1, lr)
        cp(lrb, PS[7][:, 0:256])
        for c in range(2):
            ts(gscr, lrb, lc[:, c:c + 1], None, ALU.is_gt)
            red(rk_[:, c:c + 1], gscr)
        red(gm, lrb, "max")
        ts(nm, gm, -1.0, None, ALU.mult)
        for c in range(2):
            act(em[:, c:c + 1], lc[:, c:c + 1], AF.Exp, bias=nm)
            ts(mk[:, c:c + 1], rk_[:, c:c + 1], float(TOPK), None, ALU.is_lt)
            tt(emm[:, c:c + 1], em[:, c:c + 1], mk[:, c:c + 1], ALU.mult)
        mm(PS[7][:, 256:258], c_ones, emm)
        cp(tot, PS[7][:, 256:258])
        tt(ssum, tot[:, 0:1], tot[:, 1:2], ALU.add)
        recip(sinv, ssum)
        ts(wf, emm, sinv[:, 0:1], None, ALU.mult)
        osc2 = tk[:, 256:320]
        for c in range(2):
            ts(osc2[:, c * 32:(c + 1) * 32], c_iota32, rk_[:, c:c + 1], None, ALU.is_equal)
        for c in range(2):
            mm(RV(PS[7].h[0:32, 300:301], PS[7].reg), osc2[:, c * 32:(c + 1) * 32], wf[:, c:c + 1],
               start=(c == 0), stop=(c == 1))
        cp(wslot, RV(PS[7].h[0:32, 300:301], PS[7].reg))
        for c in range(2):
            mm(RV(PS[7].h[0:32, 304:305], PS[7].reg), osc2[:, c * 32:(c + 1) * 32], c_kvals[:, c:c + 1],
               start=(c == 0), stop=(c == 1))
        cp(kslot, RV(PS[7].h[0:32, 304:305], PS[7].reg))
        for q in range(16):
            mm(PS[7][:, 308:309], RV(c_esel.h[:, q * 128:(q + 1) * 128], c_esel.reg), wslot)
            cp(s_all[:, q:q + 1], PS[7][:, 308:309])
            mm(PS[7][:, 312:313], RV(c_esel.h[:, q * 128:(q + 1) * 128], c_esel.reg), kslot)
            cp(idxf[:, q:q + 1], PS[7][:, 312:313])
        ts(idxg, idxf, 64.0, c_r64[:, 0:1], ALU.mult, ALU.add)
        cp(idxu, idxg)

    if dbg:
        dbgA = RV(nc.dram_tensor("dbgA", [128, 64], F32, kind="ExternalOutput").ap(), "d.dbgA")
        dbgB = RV(nc.dram_tensor("dbgB", [128, D], F32, kind="ExternalOutput").ap(), "d.dbgB")
        dbgC = RV(nc.dram_tensor("dbgC", [128, D], F32, kind="ExternalOutput").ap(), "d.dbgC")
        dbgD = RV(nc.dram_tensor("dbgD", [128, 64], F32, kind="ExternalOutput").ap(), "d.dbgD")
        dbgE = RV(nc.dram_tensor("dbgE", [128, S], F32, kind="ExternalOutput").ap(), "d.dbgE")
        dbgF = RV(nc.dram_tensor("dbgF", [128, D], BF16, kind="ExternalOutput").ap(), "d.dbgF")
        dbgG = RV(nc.dram_tensor("dbgG", [128, D], F32, kind="ExternalOutput").ap(), "d.dbgG")
        dbgH = RV(nc.dram_tensor("dbgH", [128, 64], F32, kind="ExternalOutput").ap(), "d.dbgH")

    topk(c_l1r, c_l1c, s1_all, idx1)
    if dbg:
        dma("sync", dbgA[:, 0:16], s1_all)
        dma("sync", dbgA[:, 16:32], idxg)
        dma("sync", dbgA[:, 32:48], tk[:, 512:528])
        dma("sync", dbgA[:, 48:64], tk[:, 0:16])
    topk(c_l2r, c_l2c, s2_all, idx2)

    a1w = PB16.take(16 * 1024, "a1w")
    b1s = [PB16.take(16 * 1024, f"b1s{p}") for p in range(2)]
    w1st = [PB16.take(512, f"w1st{p}") for p in range(2)]
    a1stg = [PF32.take(1024, f"a1stg{p}") for p in range(2)]
    for q in range(16):
        sg = a1stg[q % 2]
        idma(sg, a1t_d, idx1[:, q:q + 1], 0)
        act(RV(a1w.h[:, q * 1024:(q + 1) * 1024], a1w.reg), sg, AF.Copy, scale=s1_all[:, q:q + 1])
    for nn in range(4):
        bb = b1s[nn % 2]
        for q in range(16):
            idma(RV(bb.h[:, q * 1024:(q + 1) * 1024], bb.reg), b1t_d, idx1[:, q:q + 1], nn * 1024)
        for half in range(2):
            n = nn * 2 + half
            for i in range(8):
                ps = PS[[0, 2, 1, 3][i % 4]]
                for q in range(16):
                    mm(ps, RV(a1w.h[:, q * 1024 + i * 128: q * 1024 + (i + 1) * 128], a1w.reg),
                       RV(bb.h[:, q * 1024 + half * 512: q * 1024 + (half + 1) * 512], bb.reg),
                       start=(q == 0), stop=(q == 15))
                wst = w1st[i % 2]
                act(wst, ps, AF.Copy, scale=1.0)
                dma("sync", w1c_d[:, i * FF + n * 512: i * FF + (n + 1) * 512], wst)
    b.barrier()
    PB16.reset()
    PF32.reset()

    b2w = PB16.take(16 * 1024, "b2w")
    a2b = [PB16.take(16 * 1024, f"a2b{p}") for p in range(2)]
    w2st = [PB16.take(512, f"w2st{p}") for p in range(2)]
    b2stg = [PF32.take(1024, f"b2stg{p}") for p in range(2)]
    for q in range(16):
        sg = b2stg[q % 2]
        idma(sg, b2t_d, idx2[:, q:q + 1], 0)
        act(RV(b2w.h[:, q * 1024:(q + 1) * 1024], b2w.reg), sg, AF.Copy, scale=s2_all[:, q:q + 1])
    for band in range(4):
        ab = a2b[band % 2]
        for q in range(16):
            idma(RV(ab.h[:, q * 1024:(q + 1) * 1024], ab.reg), a2t_d, idx2[:, q:q + 1], band * 1024)
        for i in range(8):
            for n in range(2):
                ps = PS[0] if (i * 2 + n) % 2 == 0 else PS[2]
                for q in range(16):
                    mm(ps, RV(ab.h[:, q * 1024 + i * 128: q * 1024 + (i + 1) * 128], ab.reg),
                       RV(b2w.h[:, q * 1024 + n * 512: q * 1024 + (n + 1) * 512], b2w.reg),
                       start=(q == 0), stop=(q == 15))
                wst = w2st[n]
                act(wst, ps, AF.Copy, scale=1.0)
                dma("sync", w2c_d[:, (band * 8 + i) * 1024 + n * 512: (band * 8 + i) * 1024 + (n + 1) * 512], wst)
    b.barrier()
    PB16.reset()
    PF32.reset()

    wv_b = PB16.take(8 * 1024, "wv")
    wo_b = PB16.take(8 * 1024, "wo")
    wfi_b = PB16.take(16, "wfi")
    hfm = [PB16.take(1024, f"hfm{p}") for p in range(3)]
    scb = [PB16.take(128, f"scb{p}") for p in range(3)]
    vtb = [PB16.take(1024, f"vt{p}") for p in range(3)]
    rbf = PB16.take(1040, "rbf")
    brow = PB16.take(128, "brow")
    abf = [PB16.take(16, f"abf{p}") for p in range(3)]

    bvr = PB16.take(1024, "bvr")
    bor = PB16.take(1024, "bor")
    xt = PF32.take(2 * D, "xt2")
    xin4 = PF32.take(6 * D, "xin6")
    hs1 = [PF32.take(D, f"hs1{p}") for p in range(3)]
    attn4 = PF32.take(4 * D, "attn4")
    zo4 = PF32.take(4 * D, "zo4")
    sg1p = PF32.take(D, "sg1p")
    sg2p = PF32.take(D, "sg2p")
    rst = PF32.take(1040, "rst")
    rstB2 = PF32.take(1040, "rstB2")
    rstB = rstB2
    scf1 = PF32.take(128, "scf1")
    scf2 = PF32.take(128, "scf2")
    reps = PF32.take(128, "reps")

    for c in range(8):
        dma("gpsimd", RV(wv_b.h[:, c * 1024:(c + 1) * 1024], wv_b.reg), wv_d[c * 128:(c + 1) * 128, :])
        dma("gpsimd", RV(wo_b.h[:, c * 1024:(c + 1) * 1024], wo_b.reg), wo_d[c * 128:(c + 1) * 128, :])
        dma("gpsimd", RV(wfi_b.h[:, c * 2:(c + 1) * 2], wfi_b.reg), wfi_d[c * 128:(c + 1) * 128, :])
    dma("sync", gc1[:, 0:8], ln1gc_d)
    dma("sync", bc1[:, 0:8], ln1bc_d)
    dma("gpsimd", RV(bvr.h[0:1, :], bvr.reg), bv_d)
    dma("gpsimd", RV(bor.h[0:1, :], bor.reg), bo_d)
    load_row_and_bcast(bfi_d, bfib, 2)
    mset(offs[:, 0:1], 0.0)
    mset(rst[0:1, 0:1040], 0.0)
    mset(RV(rbf.h[0:1, :], rbf.reg), 0.0)
    xb_rr = xb_d.h.rearrange("(tp p) f -> p tp f", p=128)

    def pa_early(t):
        par = t % 3
        sb0 = 32 * par
        xtt = xt[:, (t % 2) * D:(t % 2 + 1) * D]
        dma("sync", xtt, x_d[t * 128:(t + 1) * 128, :])
        xint = xin4[:, (t % 6) * D:(t % 6 + 1) * D]
        m0, r0 = ln_stats(xtt, sb0 + 0)
        yield
        ts(xint, xtt, m0, r0, ALU.subtract, ALU.mult)
        yield
        m1, r1 = ln_stats(xint, sb0 + 16)
        yield
        ts(hs1[par], xint, m1, r1, ALU.subtract, ALU.mult)

    def pa_late(t):
        par = t % 3
        p4 = t % 4
        fb0 = 16 * par
        hf = hfm[par]
        transpose_affine(hs1[par], lambda c, _h=hf: RV(_h.h[:, c * 128:(c + 1) * 128], _h.reg), gc1, bc1)
        yield
        for n in range(2):
            ps = PS[2 + n]
            for c in range(8):
                mm(ps, RV(hf.h[:, c * 128:(c + 1) * 128], hf.reg),
                   RV(wv_b.h[:, c * 1024 + n * 512: c * 1024 + (n + 1) * 512], wv_b.reg),
                   start=(c == 0), stop=False)
            mm(ps, RV(c_ones1b.h[0:1, :], c_ones1b.reg),
               RV(bvr.h[0:1, n * 512:(n + 1) * 512], bvr.reg), start=False, stop=True)
            cp(RV(vtb[par].h[:, n * 512:(n + 1) * 512], vtb[par].reg), ps)
        yield
        for n in range(2):
            ps = PS[2 + n]
            for c in range(8):
                mm(ps, RV(hf.h[:, c * 128:(c + 1) * 128], hf.reg),
                   RV(wo_b.h[:, c * 1024 + n * 512: c * 1024 + (n + 1) * 512], wo_b.reg),
                   start=(c == 0), stop=False)
            mm(ps, RV(c_ones1b.h[0:1, :], c_ones1b.reg),
               RV(bor.h[0:1, n * 512:(n + 1) * 512], bor.reg), start=False, stop=True)
            cp(zo4[:, p4 * D + n * 512: p4 * D + (n + 1) * 512], ps)
        yield
        for c in range(8):
            mm(PS[7][:, 0:2], RV(hf.h[:, c * 128:(c + 1) * 128], hf.reg),
               RV(wfi_b.h[:, c * 2:(c + 1) * 2], wfi_b.reg), start=(c == 0), stop=(c == 7))
        fz = fiz[:, fb0:fb0 + 16]
        cp(fz[:, 0:2], PS[7][:, 0:2])
        tt(fz[:, 2:4], fz[:, 0:2], bfib[:, 0:2], ALU.add)
        act(fz[:, 4:6], fz[:, 2:4], AF.Abs)
        act(fz[:, 6:8], fz[:, 4:6], AF.Exp, scale=-1.0)
        act(fz[:, 8:10], fz[:, 6:8], AF.Ln, bias=1.0)
        ts(fz[:, 10:12], fz[:, 2:4], 0.0, None, ALU.min)
        tt(fz[:, 12:14], fz[:, 10:12], fz[:, 8:10], ALU.subtract)
        yield
        mm(PS[7][:, 4:5], c_tri, fz[:, 12:13])
        mm(PS[7][:, 8:9], c_ones, fz[:, 12:13])
        p2 = t % 2
        tt(fz[:, 14:15], PS[7][:, 4:5], offs[:, p2:p2 + 1], ALU.add)
        tt(ccols[:, t:t + 1], fz[:, 13:14], fz[:, 14:15], ALU.subtract)
        tt(offs[:, 1 - p2:2 - p2], offs[:, p2:p2 + 1], PS[7][:, 8:9], ALU.add)
        ts(reps, c_ones[:, 0:128], fz[:, 14:15], None, ALU.mult)
        tr(PS[6][:, 0:128], reps)
        cp(dbc3[par], PS[6][:, 0:128])

    def pa_attention(t):
        par = t % 3
        p4 = t % 4
        dsl = dbc3[par]
        ts(scf1, dsl, ccols[:, t:t + 1], None, ALU.add)
        tt(scf2, scf1, c_maskm, ALU.mult)
        tt(scf1, scf2, c_maska, ALU.add)
        sc = scb[par]
        act(sc, scf1, AF.Exp)
        yield
        mm(PS[0], sc, RV(vtb[par].h[:, 0:512], vtb[par].reg), start=True, stop=(t == 0))
        mm(PS[1], sc, RV(vtb[par].h[:, 512:1024], vtb[par].reg), start=True, stop=(t == 0))
        mm(PS[4][:, 0:1], sc, c_ocbf, start=True, stop=(t == 0))
        if t > 0:
            ts(astt[0:1, 2:3], RV(dbc3[par].h[0:1, 0:1], dbc3[par].reg), -1.0, None, ALU.mult)
            act(RV(brow.h[0:1, 0:128], brow.reg), RV(dbc3[par].h[0:1, 0:128], dbc3[par].reg),
                AF.Exp, bias=RV(astt.h[0:1, 2:3], astt.reg))
            mm(PS[0], RV(brow.h[0:1, :], brow.reg), RV(rbf.h[0:1, 0:512], rbf.reg),
               start=False, stop=True)
            mm(PS[1], RV(brow.h[0:1, :], brow.reg), RV(rbf.h[0:1, 512:1024], rbf.reg),
               start=False, stop=True)
            mm(PS[4][:, 0:1], RV(brow.h[0:1, :], brow.reg), RV(rbf.h[0:1, 1024:1025], rbf.reg),
               start=False, stop=True)
        yield
        if t + 1 < NT:
            refn = RV(dbc3[(t + 1) % 3].h[:, 0:1], dbc3[(t + 1) % 3].reg)
            act(abf[par][:, 0:1], ccols[:, t:t + 1], AF.Exp, bias=refn)
            mm(RV(PS[5].h[0:1, 0:512], PS[5].reg), abf[par][:, 0:1], RV(vtb[par].h[:, 0:512], vtb[par].reg))
            mm(RV(PS[6].h[0:1, 0:512], PS[6].reg), abf[par][:, 0:1], RV(vtb[par].h[:, 512:1024], vtb[par].reg))
            mm(RV(PS[4].h[0:1, 300:301], PS[4].reg), abf[par][:, 0:1], c_ocbf)
            tt(astt[0:1, 3:4], RV(dbc3[(t + 1) % 3].h[0:1, 0:1], dbc3[(t + 1) % 3].reg),
               RV(dbc3[par].h[0:1, 0:1], dbc3[par].reg), ALU.subtract)
            act(astt[0:1, 4:5], astt[0:1, 3:4], AF.Exp)
            yield
            ts(rstB[0:1, 0:1025], rst[0:1, 0:1025], RV(astt.h[0:1, 4:5], astt.reg), None, ALU.mult)
            tt(rst[0:1, 0:512], rstB[0:1, 0:512], RV(PS[5].h[0:1, 0:512], PS[5].reg), ALU.add)
            tt(rst[0:1, 512:1024], rstB[0:1, 512:1024], RV(PS[6].h[0:1, 0:512], PS[6].reg), ALU.add)
            tt(rst[0:1, 1024:1025], rstB[0:1, 1024:1025], RV(PS[4].h[0:1, 300:301], PS[4].reg), ALU.add)
            act(RV(rbf.h[0:1, 0:1025], rbf.reg), rst[0:1, 0:1025], AF.Copy, scale=1.0)
        ts(astt[:, 0:1], PS[4][:, 0:1], 1e-6, None, ALU.add)
        recip(astt[:, 1:2], astt[:, 0:1])
        act(attn4[:, p4 * D:p4 * D + 512], PS[0], AF.Copy, scale=astt[:, 1:2])
        act(attn4[:, p4 * D + 512:(p4 + 1) * D], PS[1], AF.Copy, scale=astt[:, 1:2])

    def pa_att2a(t):
        s0 = (t % 4) * D
        zp = zo4[:, s0:s0 + D]
        act(sg1p, zp, AF.Abs)
        yield
        act(sg2p, sg1p, AF.Exp, scale=-1.0)
        yield
        act(sg1p, sg2p, AF.Ln, bias=1.0)
        ts(sg2p, zp, 0.0, None, ALU.min)
        yield
        tt(zp, sg2p, sg1p, ALU.subtract)
        yield
        act(sg1p, zp, AF.Exp)

    def pa_att2b(t):
        s0 = (t % 4) * D
        ap_ = attn4[:, s0:s0 + D]
        xp = xin4[:, (t % 6) * D:(t % 6 + 1) * D]
        tt(sg2p, sg1p, ap_, ALU.mult, eng="gpsimd")
        tt(sg1p, sg2p, xp, ALU.add, eng="gpsimd")
        dma("gpsimd", xb_d[t * 128:(t + 1) * 128, :], sg1p)

    def run_gens(*gens):
        gens = [g for g in gens if g is not None]
        done = [False] * len(gens)
        while not all(done):
            for gi, g in enumerate(gens):
                if not done[gi]:
                    try:
                        next(g)
                    except StopIteration:
                        done[gi] = True

    run_gens(pa_early(0))
    run_gens(pa_early(1))
    run_gens(pa_early(2))
    run_gens(pa_late(0))
    run_gens(pa_early(3))
    run_gens(pa_late(1))
    for t in range(NT):
        if t >= 2:
            pa_att2b(t - 2)
        run_gens(pa_att2a(t - 1) if t >= 1 else None,
                 pa_early(t + 4) if t + 4 < NT else None,
                 pa_late(t + 2) if t + 2 < NT else None,
                 pa_attention(t))
    pa_att2b(NT - 2)
    run_gens(pa_att2a(NT - 1))
    pa_att2b(NT - 1)
    b.barrier()
    PB16.reset()
    PF32.reset()

    GW = 512
    NG = S // GW
    w1a_b = PB16.take(8 * HA, "w1a")
    h2fm = [PB16.take(8 * GW, f"h2fm{p}") for p in range(2)]
    mst = [PB16.take(24 * GW, "mst")]

    gc2 = PF32.take(16, "gc2")
    bc2 = PF32.take(16, "bc2")
    b1cs = PF32.take(32, "b1c")
    xbg = [PF32.take(4 * D, f"xbg{p}") for p in range(2)]
    h2s = PF32.take(D, "h2s")
    zbb = [PF32.take(GW, f"zb{p}") for p in range(2)]
    lkb = [PF32.take(GW, f"lk{p}") for p in range(2)]
    rkb = [PF32.take(GW, f"rk{p}") for p in range(2)]

    for c in range(8):
        dma("gpsimd", RV(w1a_b.h[:, c * HA:(c + 1) * HA], w1a_b.reg), w1a_d[c * 128:(c + 1) * 128, :])
    dma("sync", gc2[:, 0:8], ln2gc_d)
    dma("sync", bc2[:, 0:8], ln2bc_d)
    dma("sync", b1cs[:, 0:24], b1c_d)

    xb_r = xb_d.h.rearrange("(tp p) f -> p tp f", p=128)

    def pb1_prologue(g):
        par = g % 2
        nb = xbg[par]
        dma("sync", RV(nb.h.rearrange("p (i f) -> p i f", i=4), nb.reg),
            RV(xb_r[:, g * 4:(g + 1) * 4, :], xb_d.reg))
        for i in range(4):
            xbt = xbg[par][:, i * 1024:(i + 1) * 1024]
            m2, r2 = ln_stats(xbt, 32 * par)
            yield
            ts(h2s, xbt, m2, r2, ALU.subtract, ALU.mult)
            yield
            hfmt = h2fm[par]
            transpose_affine(h2s, lambda c, _i=i, _h=hfmt: RV(_h.h[:, c * GW + _i * 128: c * GW + (_i + 1) * 128], _h.reg), gc2, bc2)
            yield

    def pb1_main(g):
        par = g % 2
        for fb in range(24):
            if fb % 2 == 0:
                yield
            ps = PS[[0, 2, 1, 3][fb % 4]]
            for c in range(8):
                mm(ps, RV(w1a_b.h[:, c * HA + fb * 128: c * HA + (fb + 1) * 128], w1a_b.reg),
                   RV(h2fm[par].h[:, c * GW:(c + 1) * GW], h2fm[par].reg), start=(c == 0), stop=(c == 7))
            sgn, zsq, fac = zbb[fb % 2], lkb[fb % 2], rkb[fb % 2]
            act(sgn, ps, AF.Sign, bias=b1cs[:, fb:fb + 1])
            act(zsq, ps, AF.Square, bias=b1cs[:, fb:fb + 1])
            ts(fac, sgn, 0.49995000000223516, 0.5000499999977648, ALU.mult, ALU.add)
            tt(RV(mst[0].h[:, fb * GW:(fb + 1) * GW], mst[0].reg), zsq, fac, ALU.mult)
            if fb == 11:
                dma("gpsimd", m_d[:, g * 24 * GW: g * 24 * GW + 12 * GW],
                    RV(mst[0].h[:, 0:12 * GW], mst[0].reg))
        dma("gpsimd", m_d[:, g * 24 * GW + 12 * GW:(g + 1) * 24 * GW],
            RV(mst[0].h[:, 12 * GW:24 * GW], mst[0].reg))

    run_gens(pb1_prologue(0))
    for g in range(NG):
        run_gens(pb1_prologue(g + 1) if g + 1 < NG else None, pb1_main(g))
    b.barrier()
    PB16.reset()
    PF32.reset()

    w2a_b = PB16.take(24 * 1024, "w2a")
    mgrp = [PB16.take(24 * GW, f"mgrp{p}") for p in range(2)]

    b2ab = PF32.take(D, "b2ab")
    xbg2 = [PF32.take(4 * D, f"xbg2{p}") for p in range(2)]
    xt2 = [PF32.take(D, f"xt2{p}") for p in range(2)]
    ybuf = [PF32.take(D, f"ybuf{p}") for p in range(2)]
    yb2 = PF32.take(D, "yb2")
    lns = PF32.take(D, "lns")
    x1st = [PF32.take(D, f"x1st{p}") for p in range(2)]

    for q in range(24):
        dma("gpsimd", RV(w2a_b.h[:, q * 1024:(q + 1) * 1024], w2a_b.reg), w2a_d[q * 128:(q + 1) * 128, :])
    load_row_and_bcast(b2a_d, b2ab, D)

    def pb2_loads(g):
        par = g % 2
        dma("sync", mgrp[par], m_d[:, g * 24 * GW:(g + 1) * 24 * GW])
        nb = xbg2[par]
        dma("sync", RV(nb.h.rearrange("p (i f) -> p i f", i=4), nb.reg),
            RV(xb_r[:, g * 4:(g + 1) * 4, :], xb_d.reg))

    def pb2_epi(g, i):
        par = g % 2
        tglob = g * 4 + i
        tp2 = tglob % 2
        yb = ybuf[i % 2]
        tt(yb2, yb, xbg2[par][:, i * 1024:(i + 1) * 1024], ALU.add)
        yield
        mo, ro = ln_stats(yb2, 32 * (i % 2))
        yield
        ts(lns, yb2, mo, ro, ALU.subtract, ALU.mult)
        yield
        tt(x1st[i % 2], lns, xt2[tp2], ALU.add)
        dma("gpsimd", x1_d[tglob * 128:(tglob + 1) * 128, :], x1st[i % 2])

    def pb2_mms(g, i):
        par = g % 2
        tglob = g * 4 + i
        tp2 = tglob % 2
        dma("gpsimd", xt2[tp2], x_d[tglob * 128:(tglob + 1) * 128, :])
        for n in range(2):
            ps = PS[1 + n] if i % 2 == 0 else PS[5 + n]
            for fb in range(24):
                if fb % 8 == 0:
                    yield
                mm(ps, RV(mgrp[par].h[:, fb * GW + i * 128: fb * GW + (i + 1) * 128], mgrp[par].reg),
                   RV(w2a_b.h[:, fb * 1024 + n * 512: fb * 1024 + (n + 1) * 512], w2a_b.reg),
                   start=(fb == 0), stop=(fb == 23))
            tt(ybuf[i % 2][:, n * 512:(n + 1) * 512], ps, b2ab[:, n * 512:(n + 1) * 512], ALU.add)

    pb2_loads(0)
    for g in range(NG):
        for i in range(4):
            other = pb2_epi(g, i - 1) if i >= 1 else (pb2_epi(g - 1, 3) if g >= 1 else None)
            run_gens(pb2_mms(g, i), other)
        if g + 1 < NG:
            pb2_loads(g + 1)
    run_gens(pb2_epi(NG - 1, 3))
    b.barrier()
    PB16.reset()
    PF32.reset()

    GW1 = 512
    NG1 = S // GW1
    w1c_b = PB16.take(8 * FF, "w1c")
    xnfm = [PB16.take(8 * GW1, f"xnfm{p}") for p in range(2)]
    gst = [PB16.take(4096, f"gst{p}") for p in range(2)]
    x1g = [PF32.take(4 * D, f"x1g{p}") for p in range(2)]
    xns = PF32.take(D, "xns")

    dma("sync", w1c_b, RV(w1c_d.h[:, :], w1c_d.reg))
    x1_r = x1_d.h.rearrange("(tp p) f -> p tp f", p=128)

    def pc1_prologue(g):
        par = g % 2
        nb = x1g[par]
        dma("sync", RV(nb.h.rearrange("p (i f) -> p i f", i=4), nb.reg),
            RV(x1_r[:, g * 4:(g + 1) * 4, :], x1_d.reg))
        for i in range(4):
            x1t = x1g[par][:, i * 1024:(i + 1) * 1024]
            mn, rn = ln_stats(x1t, 32 * par, via_pow=True)
            yield
            ts(xns, x1t, mn, rn, ALU.subtract, ALU.mult)
            yield
            xf = xnfm[par]
            transpose_tile(xns, lambda c, _i=i, _x=xf: RV(_x.h[:, c * GW1 + _i * 128: c * GW1 + (_i + 1) * 128], _x.reg))
            yield

    def pc1_main(g):
        par = g % 2
        for fb in range(32):
            if fb % 3 == 0:
                yield
            ps = PS[[0, 2, 1, 3][fb % 4]]
            for c in range(8):
                mm(ps, RV(w1c_b.h[:, c * FF + fb * 128: c * FF + (fb + 1) * 128], w1c_b.reg),
                   RV(xnfm[par].h[:, c * GW1:(c + 1) * GW1], xnfm[par].reg), start=(c == 0), stop=(c == 7))
            gs = gst[(fb // 8) % 2]
            act(RV(gs.h[:, (fb % 8) * 512:(fb % 8 + 1) * 512], gs.reg), ps, AF.Gelu_apprx_tanh)
            if fb % 8 == 7:
                dma("gpsimd", g_d[:, g * 16384 + (fb - 7) * 512: g * 16384 + (fb + 1) * 512], gs)

    run_gens(pc1_prologue(0))
    for g in range(NG1):
        run_gens(pc1_prologue(g + 1) if g + 1 < NG1 else None, pc1_main(g))
    b.barrier()
    PB16.reset()
    PF32.reset()

    w2c_b = PB16.take(32 * 1024, "w2c")
    gq = [PB16.take(4096, f"gq{q}") for q in range(4)]
    x1g2 = [PF32.take(4 * D, f"x1g2{p}") for p in range(2)]
    ybufg = [PF32.take(4 * D, f"ybufg{p}") for p in range(2)]
    lns2 = PF32.take(D, "lns2")
    outst = [PF32.take(D, f"outst{p}") for p in range(2)]

    dma("sync", w2c_b, RV(w2c_d.h[:, :], w2c_d.reg))
    for qd in range(4):
        dma("sync", gq[qd], RV(g_d.h[:, qd * 4096:(qd + 1) * 4096], g_d.reg))

    def pc2_epi(g):
        gp = g % 2
        for i in range(4):
            tglob = g * 4 + i
            yb = ybufg[gp][:, i * 1024:(i + 1) * 1024]
            mo, ro = ln_stats(yb, 32 * (i % 2), via_pow=True)
            yield
            ts(lns2, yb, mo, ro, ALU.subtract, ALU.mult)
            yield
            tt(outst[i % 2], lns2, x1g2[gp][:, i * 1024:(i + 1) * 1024], ALU.add)
            dma("gpsimd", out_d[tglob * 128:(tglob + 1) * 128, :], outst[i % 2])
            yield

    def pc2_mm(g):
        gp = g % 2
        dma("gpsimd", RV(x1g2[gp].h.rearrange("p (i f) -> p i f", i=4), x1g2[gp].reg),
            RV(x1_r[:, g * 4:(g + 1) * 4, :], x1_d.reg))
        for qd in range(4):
            if qd >= 1 and g + 1 < NG1:
                dma("sync", gq[qd - 1],
                    RV(g_d.h[:, (g + 1) * 16384 + (qd - 1) * 4096: (g + 1) * 16384 + qd * 4096], g_d.reg))
            for k in range(8):
                fb = qd * 8 + k
                if k % 3 == 0:
                    yield
                for i in range(4):
                    for n in range(2):
                        mm(PS[i * 2 + n],
                           RV(gq[qd].h[:, k * 512 + i * 128: k * 512 + (i + 1) * 128], gq[qd].reg),
                           RV(w2c_b.h[:, fb * 1024 + n * 512: fb * 1024 + (n + 1) * 512], w2c_b.reg),
                           start=(fb == 0), stop=(fb == 31))
        if g + 1 < NG1:
            dma("sync", gq[3],
                RV(g_d.h[:, (g + 1) * 16384 + 3 * 4096: (g + 2) * 16384], g_d.reg))
        for i in range(4):
            cp(ybufg[gp][:, i * 1024:i * 1024 + 512], PS[i * 2])
            cp(ybufg[gp][:, i * 1024 + 512:(i + 1) * 1024], PS[i * 2 + 1])

    for g in range(NG1):
        run_gens(pc2_mm(g), pc2_epi(g - 1) if g >= 1 else None)
    run_gens(pc2_epi(NG1 - 1))
    b.barrier()

    b.finalize()
    with nc.Block() as block:
        b.play(block, sems)
    st.close()
    return nc


def _host_common(inputs):
    g = lambda k: np.ascontiguousarray(np.asarray(inputs[k], dtype=np.float32))
    i = np.arange(128)
    esel = np.zeros((32, 2048), np.float32)
    for q in range(16):
        for p in range(128):
            esel[2 * q + p // 64, q * 128 + p] = 1.0
    wf = g("L0_attn_Wf")
    wi = g("L0_attn_Wi")
    return {
        "wv": g("L0_attn_Wv"),
        "wo": g("L0_attn_Wo"),
        "wfi": np.ascontiguousarray(np.concatenate([wf, wi], axis=1)),
        "w1a": g("L0_attn_W1"),
        "w2a": g("L0_attn_W2"),
        "bv": g("L0_attn_bv").reshape(1, D),
        "bo": g("L0_attn_bo").reshape(1, D),
        "bfi": np.ascontiguousarray(
            np.array([[np.asarray(inputs["L0_attn_bf"]).reshape(-1)[0],
                       np.asarray(inputs["L0_attn_bi"]).reshape(-1)[0]]], dtype=np.float32)),
        "g1": g("L0_attn_ln1_g").reshape(1, D),
        "b1": g("L0_attn_ln1_b").reshape(1, D),
        "g2": g("L0_attn_ln2_g").reshape(1, D),
        "b2": g("L0_attn_ln2_b").reshape(1, D),
        "b1c": np.ascontiguousarray(g("L0_attn_b1").reshape(24, 128).T),
        "ln1gc": np.ascontiguousarray(g("L0_attn_ln1_g").reshape(8, 128).T),
        "ln1bc": np.ascontiguousarray(g("L0_attn_ln1_b").reshape(8, 128).T),
        "ln2gc": np.ascontiguousarray(g("L0_attn_ln2_g").reshape(8, 128).T),
        "ln2bc": np.ascontiguousarray(g("L0_attn_ln2_b").reshape(8, 128).T),
        "b2a": g("L0_attn_b2").reshape(1, D),
        "a1t": np.ascontiguousarray(np.asarray(inputs["L0_ffn_fc1_A"], np.float32).transpose(0, 2, 1).reshape(NEXP * RK, D)),
        "b1t": np.ascontiguousarray(np.asarray(inputs["L0_ffn_fc1_B"], np.float32).reshape(NEXP * RK, FF)),
        "a2t": np.ascontiguousarray(np.asarray(inputs["L0_ffn_fc2_A"], np.float32).transpose(0, 2, 1).reshape(NEXP * RK, FF)),
        "b2t": np.ascontiguousarray(np.asarray(inputs["L0_ffn_fc2_B"], np.float32).reshape(NEXP * RK, D)),
        "l1r": g("L0_ffn_fc1_logits").reshape(1, NEXP),
        "l1c": np.ascontiguousarray(g("L0_ffn_fc1_logits").reshape(2, 128).T),
        "l2r": g("L0_ffn_fc2_logits").reshape(1, NEXP),
        "l2c": np.ascontiguousarray(g("L0_ffn_fc2_logits").reshape(2, 128).T),
        "tri": np.ascontiguousarray(np.triu(np.ones((128, 128), np.float32))),
        "onesq": np.ones((128, 128), np.float32),
        "ones1": np.ones((1, 128), np.float32),
        "idf": np.ascontiguousarray(np.eye(128, dtype=np.float32)),
        "maskm": np.ascontiguousarray((i[:, None] <= i[None, :]).astype(np.float32)),
        "maska": np.ascontiguousarray(np.where(i[:, None] <= i[None, :], 0.0, -100.0).astype(np.float32)),
        "iota32": np.ascontiguousarray(np.broadcast_to(np.arange(32, dtype=np.float32)[None, :], (128, 32)).copy()),
        "kvals": np.ascontiguousarray(np.stack([i, i + 128], axis=1).astype(np.float32)),
        "r64": (i % 64).astype(np.float32).reshape(128, 1),
        "esel": esel,
        "ocbf": np.ones((128, 1), ml_dtypes.bfloat16),
        "ones1b": np.ones((1, 128), ml_dtypes.bfloat16),
    }


def kernel(**inputs):
    x = np.asarray(inputs["L0_attn_x"], dtype=np.float32)
    common = _host_common(inputs)
    nc = build_program()
    in_maps = [dict(common, x=np.ascontiguousarray(x[c])) for c in range(8)]
    res = run_bass_kernel_spmd(nc, in_maps, core_ids=list(range(8)))
    return np.stack([np.asarray(res.results[c]["out"]) for c in range(8)], axis=0)


# version 20
# speedup vs baseline: 1.0422x; 1.0422x over previous
import numpy as np
import ml_dtypes
from contextlib import ExitStack

import concourse.bass as bass
import concourse.mybir as mybir
from concourse.bass_utils import run_bass_kernel_spmd

F32 = mybir.dt.float32
BF16 = mybir.dt.bfloat16
U32 = mybir.dt.uint32
AF = mybir.ActivationFunctionType
ALU = mybir.AluOpType
AX = mybir.AxisListType
f32 = np.float32

ENGINES = ("tensor", "vector", "scalar", "gpsimd", "sync")

S, D, NT = 4096, 1024, 32
HA = 3072
FF = 4096
NEXP, RK, TOPK = 256, 64, 32
EPS = 1e-5


class RV:
    __slots__ = ("h", "reg")

    def __init__(self, h, reg):
        self.h = h
        self.reg = reg

    def __getitem__(self, k):
        return RV(self.h[k], self.reg)

    @property
    def ap(self):
        return self.h


class Builder:
    def __init__(self, nc):
        self.nc = nc
        self.ops = []
        self.rw = {}
        self.rr = {}

    def op(self, eng, fn, reads=(), writes=(), dma=False):
        idx = len(self.ops)
        node = ("dma", idx) if dma else ("op", idx)
        deps = set()
        for r in reads:
            w = self.rw.get(r)
            if w is not None:
                deps.add(w)
        for r in writes:
            w = self.rw.get(r)
            if w is not None:
                deps.add(w)
            for rd in self.rr.get(r, ()):
                deps.add(rd)
        deps.discard(node)
        self.ops.append(dict(idx=idx, eng=eng, fn=fn, dma=dma, deps=deps))
        for r in reads:
            self.rr.setdefault(r, []).append(node)
        for r in writes:
            self.rw[r] = node
            self.rr[r] = []
        return idx

    def barrier(self):
        last = {}
        for i, o in enumerate(self.ops):
            if o["fn"] is None and not o["dma"]:
                continue
            key = (o["eng"], o["dma"])
            last[key] = ("dma", i) if o["dma"] else ("op", i)
        nodes = set(last.values())
        for e in ENGINES:
            deps = set()
            for n in nodes:
                if n[0] == "op" and self.ops[n[1]]["eng"] == e:
                    continue
                deps.add(n)
            self.ops.append(dict(idx=len(self.ops), eng=e, fn=None, dma=False, deps=deps))
        self.rw.clear()
        self.rr.clear()

    def finalize(self):
        need = set()
        for o in self.ops:
            for n in o["deps"]:
                if n[0] == "op":
                    if self.ops[n[1]]["eng"] == o["eng"] == "tensor":
                        continue
                    need.add(n[1])
        inc_no, dma_no = {}, {}
        cnt = {e: 0 for e in ENGINES}
        dcnt = {e: 0 for e in ENGINES}
        for i, o in enumerate(self.ops):
            if o["dma"]:
                dcnt[o["eng"]] += 1
                dma_no[i] = dcnt[o["eng"]] * 16
            elif i in need:
                cnt[o["eng"]] += 1
                inc_no[i] = cnt[o["eng"]]
        VC = {e: {} for e in ENGINES}
        lastinc = {e: 0 for e in ENGINES}
        snap = {}
        for i, o in enumerate(self.ops):
            e = o["eng"]
            waits = []
            for n in sorted(o["deps"]):
                j = n[1]
                te = self.ops[j]["eng"]
                if n[0] == "op":
                    if te == e == "tensor":
                        continue
                    key, tc = te, inc_no[j]
                else:
                    key, tc = te + "*", dma_no[j]
                if VC[e].get(key, 0) >= tc:
                    continue
                waits.append((key, tc))
                VC[e][key] = tc
                for k2, v2 in snap.get(n, {}).items():
                    if VC[e].get(k2, 0) < v2:
                        VC[e][k2] = v2
            o["waits"] = waits
            if o["dma"]:
                o["selfwait"] = dma_no[i]
                VC[e][e + "*"] = dma_no[i]
                s = dict(VC[e])
                s[e] = max(s.get(e, 0), lastinc[e])
                snap[("dma", i)] = s
            elif i in inc_no:
                lastinc[e] = inc_no[i]
                s = dict(VC[e])
                s[e] = inc_no[i]
                snap[("op", i)] = s
        self.inc_no = inc_no

    def play(self, block, sems):
        inc_no = self.inc_no
        for ename in ENGINES:
            ops = [o for o in self.ops if o["eng"] == ename]

            def body(eng, _ops=ops, _e=ename):
                for o in _ops:
                    for key, val in o["waits"]:
                        eng.wait_ge(sems[key], val)
                    if o["fn"] is None:
                        continue
                    inst = o["fn"](eng)
                    if o["dma"]:
                        inst.then_inc(sems[_e + "*"], 16)
                        eng.wait_ge(sems[_e + "*"], o["selfwait"])
                    elif o["idx"] in inc_no:
                        inst.then_inc(sems[_e], 1)

            getattr(block, ename)(body)


class Pool:
    def __init__(self, handle, ncols, tag):
        self.h = handle
        self.n = ncols
        self.tag = tag
        self.o = 0
        self.cnt = 0

    def take(self, cols, name):
        o = (self.o + 15) & ~15
        assert o + cols <= self.n, f"pool {self.tag} overflow: {name} needs {cols} at {o}/{self.n}"
        self.o = o + cols
        self.cnt += 1
        return RV(self.h[:, o:o + cols], f"{self.tag}.{name}.{self.cnt}")

    def reset(self):
        self.o = 0


def build_program(dbg=False):
    nc = bass.Bass("TRN2", target_bir_lowering=False)
    b = Builder(nc)

    def din(name, shape, dt=F32):
        return RV(nc.dram_tensor(name, shape, dt, kind="ExternalInput").ap(), "d." + name)

    def dint(name, shape, dt=F32):
        if dbg:
            return RV(nc.dram_tensor(name, shape, dt, kind="ExternalOutput").ap(), "d." + name)
        return RV(nc.dram_tensor(name, shape, dt).ap(), "d." + name)

    x_d = din("x", [S, D])
    wv_d = din("wv", [D, D])
    wo_d = din("wo", [D, D])
    wfi_d = din("wfi", [D, 2])
    w1a_d = din("w1a", [D, HA])
    w2a_d = din("w2a", [HA, D])
    bv_d = din("bv", [1, D])
    bo_d = din("bo", [1, D])
    bfi_d = din("bfi", [1, 2])
    g1_d = din("g1", [1, D])
    b1_d = din("b1", [1, D])
    g2_d = din("g2", [1, D])
    b2_d = din("b2", [1, D])
    b1c_d = din("b1c", [128, 24])
    ln1gc_d = din("ln1gc", [128, 8])
    ln1bc_d = din("ln1bc", [128, 8])
    ln2gc_d = din("ln2gc", [128, 8])
    ln2bc_d = din("ln2bc", [128, 8])
    b2a_d = din("b2a", [1, D])
    a1t_d = din("a1t", [NEXP * RK, D])
    b1t_d = din("b1t", [NEXP * RK, FF])
    a2t_d = din("a2t", [NEXP * RK, FF])
    b2t_d = din("b2t", [NEXP * RK, D])
    l1r_d = din("l1r", [1, NEXP])
    l1c_d = din("l1c", [128, 2])
    l2r_d = din("l2r", [1, NEXP])
    l2c_d = din("l2c", [128, 2])
    tri_d = din("tri", [128, 128])
    onesq_d = din("onesq", [128, 128])
    ones1_d = din("ones1", [1, 128])
    idf_d = din("idf", [128, 128])
    maskm_d = din("maskm", [128, 128])
    maska_d = din("maska", [128, 128])
    iota32_d = din("iota32", [128, 32])
    kvals_d = din("kvals", [128, 2])
    r64_d = din("r64", [128, 1])
    esel_d = din("esel", [32, 2048])
    ocbf_d = din("ocbf", [128, 1], BF16)
    ones1b_d = din("ones1b", [1, 128], BF16)

    out_d = RV(nc.dram_tensor("out", [S, D], F32, kind="ExternalOutput").ap(), "d.out")
    xb_d = dint("xb_scr", [S, D])
    x1_d = dint("x1_scr", [S, D])
    w1c_d = dint("w1c_scr", [128, 8 * FF], BF16)
    w2c_d = dint("w2c_scr", [128, 32 * D], BF16)
    g_d = dint("g_scr", [128, 131072], BF16)
    m_d = dint("m_scr", [128, 98304], BF16)

    st = ExitStack()

    def sb(name, shape, dt=F32):
        t = st.enter_context(nc.sbuf_tensor(name, shape, dt))
        return RV(t[tuple(slice(None) for _ in shape)], name)

    c_tri = sb("c_tri", [128, 128])
    c_ones = sb("c_ones", [128, 128])
    c_ones1 = sb("c_ones1", [1, 128])
    c_idf = sb("c_idf", [128, 128])
    c_maskm = sb("c_maskm", [128, 128])
    c_maska = sb("c_maska", [128, 128])
    c_iota32 = sb("c_iota32", [128, 32])
    c_kvals = sb("c_kvals", [128, 2])
    c_r64 = sb("c_r64", [128, 1])
    c_ocbf = sb("c_ocbf", [128, 1], BF16)
    c_ones1b = sb("c_ones1b", [1, 128], BF16)
    c_l1r = sb("c_l1r", [1, NEXP])
    c_l1c = sb("c_l1c", [128, 2])
    c_l2r = sb("c_l2r", [1, NEXP])
    c_l2c = sb("c_l2c", [128, 2])

    ccols = sb("ccols", [128, NT])
    dbc3 = [sb(f"dbc{p}", [128, 128]) for p in range(3)]
    gc1 = sb("gc1", [128, 8])
    bc1 = sb("bc1", [128, 8])
    bfib = sb("bfib", [128, 2])
    offs = sb("offs", [128, 2])
    stt = sb("stt", [128, 96])
    astt = sb("astt", [128, 8])
    fiz = sb("fiz", [128, 48])
    powc = sb("powc", [128, 1])
    s1_all = sb("s1_all", [128, 16])
    s2_all = sb("s2_all", [128, 16])
    idx1 = sb("idx1", [128, 16], U32)
    idx2 = sb("idx2", [128, 16], U32)
    rows = sb("rows", [1, 512])

    PB16 = Pool(st.enter_context(nc.sbuf_tensor("PB16", [128, 50400], BF16)), 50400, "b")
    PF32 = Pool(st.enter_context(nc.sbuf_tensor("PF32", [128, 25408], F32)), 25408, "f")

    PS = [RV(st.enter_context(nc.psum_tensor(f"PS{i}", [128, 512], F32))[:, :], f"PS{i}")
          for i in range(8)]

    sems = {}
    for e in ENGINES:
        sems[e] = st.enter_context(nc.semaphore("sem_" + e))
    for e in ("sync", "gpsimd", "scalar"):
        sems[e + "*"] = st.enter_context(nc.semaphore("semd_" + e))

    def mm(out, lhsT, rhs, start=True, stop=True):
        b.op("tensor",
             lambda e, o=out.ap, l=lhsT.ap, r=rhs.ap, s=start, t=stop: e.matmul(o, l, r, start=s, stop=t),
             reads=[lhsT.reg, rhs.reg], writes=[out.reg])

    def tr(out, in_):
        b.op("tensor", lambda e, o=out.ap, i=in_.ap, d=c_idf.ap: e.transpose(o, i, d),
             reads=[in_.reg, "c_idf"], writes=[out.reg])

    def act(out, in_, func, bias=None, scale=None, accum=None):
        kw = {}
        rd = [in_.reg]
        if bias is not None:
            if isinstance(bias, RV):
                kw["bias"] = bias.ap
                rd.append(bias.reg)
            else:
                kw["bias"] = f32(bias)
        if scale is not None:
            if isinstance(scale, RV):
                kw["scale"] = scale.ap
                rd.append(scale.reg)
            else:
                kw["scale"] = f32(scale)
        wr = [out.reg]
        if accum is not None:
            kw["accum_out"] = accum.ap
            wr.append(accum.reg)
        b.op("scalar", lambda e, o=out.ap, i=in_.ap, f=func, k=kw: e.activation(o, i, f, **k),
             reads=rd, writes=wr)

    def ts(out, in0, s1, s2, op0, op1=None, eng="vector"):
        rd = [in0.reg]
        a1, a2 = s1, s2
        if isinstance(s1, RV):
            a1 = s1.ap
            rd.append(s1.reg)
        if isinstance(s2, RV):
            a2 = s2.ap
            rd.append(s2.reg)
        kw = {}
        if op1 is not None:
            kw["op1"] = op1
        b.op(eng, lambda e, o=out.ap, i=in0.ap, x=a1, y=a2, p0=op0, k=kw: e.tensor_scalar(o, i, x, y, op0=p0, **k),
             reads=rd, writes=[out.reg])

    def tt(out, in0, in1, op, eng="vector"):
        b.op(eng, lambda e, o=out.ap, i=in0.ap, j=in1.ap, p=op: e.tensor_tensor(o, i, j, op=p),
             reads=[in0.reg, in1.reg], writes=[out.reg])

    def red(out, in_, which="sum"):
        fn = "reduce_sum" if which == "sum" else "reduce_max"
        b.op("vector", lambda e, o=out.ap, i=in_.ap, f=fn: getattr(e, f)(o, i, axis=AX.X),
             reads=[in_.reg], writes=[out.reg])

    def cp(out, in_, eng="vector"):
        b.op(eng, lambda e, o=out.ap, i=in_.ap: e.tensor_copy(o, i),
             reads=[in_.reg], writes=[out.reg])

    def recip(out, in_):
        b.op("vector", lambda e, o=out.ap, i=in_.ap: e.reciprocal(o, i),
             reads=[in_.reg], writes=[out.reg])

    def mset(t, val, eng="vector"):
        b.op(eng, lambda e, o=t.ap, v=val: e.memset(o, v), reads=[], writes=[t.reg])

    def dma(q, out, in_):
        b.op(q, lambda e, o=out.ap, i=in_.ap: e.dma_start(o, i),
             reads=[in_.reg], writes=[out.reg], dma=True)

    def idma(out, table, idx, eloff):
        b.op("gpsimd",
             lambda e, o=out.ap, t=table.ap, ix=idx.ap, el=eloff:
             e.indirect_dma_start(o, None, t, bass.IndirectOffsetOnAxis(ix, 0), element_offset=el),
             reads=[table.reg, idx.reg], writes=[out.reg], dma=True)

    def load_row_and_bcast(row_dram, out, width):
        for n0 in range(0, width, 512):
            w = min(512, width - n0)
            dma("sync", rows[:, 0:w], row_dram[:, n0:n0 + w])
            mm(PS[7][:, 0:w], c_ones1, rows[:, 0:w])
            cp(out[:, n0:n0 + w], PS[7][:, 0:w])

    def ln_stats(xrv, c0, via_pow=False):
        b.op("vector", lambda e, o=stt.h[:, c0 + 4:c0 + 10], i=xrv[:, 0:512].ap: e.bn_stats(o, i),
             reads=[xrv.reg], writes=["stt"])
        b.op("vector", lambda e, o=stt.h[:, c0 + 10:c0 + 16], i=xrv[:, 512:1024].ap: e.bn_stats(o, i),
             reads=[xrv.reg], writes=["stt"])
        b.op("vector", lambda e, o=stt.h[:, c0:c0 + 2], i=stt.h[:, c0 + 4:c0 + 16]: e.bn_aggr(o, i),
             reads=["stt"], writes=["stt"])
        mean = stt[:, c0:c0 + 1]
        var = stt[:, c0 + 1:c0 + 2]
        rstd = stt[:, c0 + 2:c0 + 3]
        if via_pow:
            ts(stt[:, c0 + 3:c0 + 4], var, EPS, None, ALU.add)
            tt(rstd, stt[:, c0 + 3:c0 + 4], powc, ALU.pow, eng="gpsimd")
        else:
            act(stt[:, c0 + 3:c0 + 4], var, AF.Ln, bias=EPS)
            act(rstd, stt[:, c0 + 3:c0 + 4], AF.Exp, scale=-0.5)
        return mean, rstd

    def transpose_tile(xrv, dst_fn):
        for gph in range(2):
            ps = PS[5 + gph]
            for k in range(4):
                c = gph * 4 + k
                tr(ps[:, k * 128:(k + 1) * 128], xrv[:, c * 128:(c + 1) * 128])
            for k in range(4):
                c = gph * 4 + k
                cp(dst_fn(c), ps[:, k * 128:(k + 1) * 128])

    def transpose_affine(xrv, dst_fn, gc, bc):
        for gph in range(2):
            ps = PS[5 + gph]
            for k in range(4):
                c = gph * 4 + k
                tr(ps[:, k * 128:(k + 1) * 128], xrv[:, c * 128:(c + 1) * 128])
            for k in range(4):
                c = gph * 4 + k
                act(dst_fn(c), ps[:, k * 128:(k + 1) * 128], AF.Identity,
                    scale=gc[:, c:c + 1], bias=bc[:, c:c + 1])

    def run_gens(*gens):
        gens = [g for g in gens if g is not None]
        done = [False] * len(gens)
        while not all(done):
            for gi, g in enumerate(gens):
                if not done[gi]:
                    try:
                        next(g)
                    except StopIteration:
                        done[gi] = True

    dma("sync", c_tri, tri_d)
    dma("sync", c_ones, onesq_d)
    dma("sync", c_ones1, ones1_d)
    dma("sync", c_idf, idf_d)
    dma("sync", c_maskm, maskm_d)
    dma("sync", c_maska, maska_d)
    dma("sync", c_iota32, iota32_d)
    dma("sync", c_kvals, kvals_d)
    dma("sync", c_r64, r64_d)
    dma("sync", c_ocbf, ocbf_d)
    dma("sync", c_ones1b, ones1b_d)
    dma("sync", c_l1r, l1r_d)
    dma("sync", c_l1c, l1c_d)
    dma("sync", c_l2r, l2r_d)
    dma("sync", c_l2c, l2c_d)
    mset(powc, -0.5, eng="gpsimd")

    tk = PF32.take(656, "tk")
    idxf = PF32.take(16, "idxf")
    idxg = PF32.take(16, "idxg")
    esel_p = PF32.take(2048, "esel")
    c_esel = RV(esel_p.h[0:32, :], esel_p.reg)
    dma("sync", c_esel, esel_d)

    def topk(lr, lc, s_all, idxu):
        lrb = tk[:, 0:256]
        gscr = tk[:, 256:512]
        rk_ = tk[:, 512:514]
        gm = tk[:, 514:515]
        nm = tk[:, 515:516]
        em = tk[:, 516:518]
        mk = tk[:, 518:520]
        emm = tk[:, 520:522]
        tot = tk[:, 522:524]
        ssum = tk[:, 524:525]
        sinv = tk[:, 525:526]
        wf = tk[:, 526:528]
        wslot = RV(tk.h[0:32, 528:529], tk.reg)
        kslot = RV(tk.h[0:32, 529:530], tk.reg)
        mm(PS[7][:, 0:256], c_ones1, lr)
        cp(lrb, PS[7][:, 0:256])
        yield
        for c in range(2):
            ts(gscr, lrb, lc[:, c:c + 1], None, ALU.is_gt)
            red(rk_[:, c:c + 1], gscr)
            yield
        red(gm, lrb, "max")
        ts(nm, gm, -1.0, None, ALU.mult)
        for c in range(2):
            act(em[:, c:c + 1], lc[:, c:c + 1], AF.Exp, bias=nm)
            ts(mk[:, c:c + 1], rk_[:, c:c + 1], float(TOPK), None, ALU.is_lt)
            tt(emm[:, c:c + 1], em[:, c:c + 1], mk[:, c:c + 1], ALU.mult)
        mm(PS[7][:, 256:258], c_ones, emm)
        cp(tot, PS[7][:, 256:258])
        tt(ssum, tot[:, 0:1], tot[:, 1:2], ALU.add)
        recip(sinv, ssum)
        yield
        ts(wf, emm, sinv[:, 0:1], None, ALU.mult)
        osc2 = tk[:, 256:320]
        for c in range(2):
            ts(osc2[:, c * 32:(c + 1) * 32], c_iota32, rk_[:, c:c + 1], None, ALU.is_equal)
        for c in range(2):
            mm(RV(PS[7].h[0:32, 300:301], PS[7].reg), osc2[:, c * 32:(c + 1) * 32], wf[:, c:c + 1],
               start=(c == 0), stop=(c == 1))
        cp(wslot, RV(PS[7].h[0:32, 300:301], PS[7].reg))
        for c in range(2):
            mm(RV(PS[7].h[0:32, 304:305], PS[7].reg), osc2[:, c * 32:(c + 1) * 32], c_kvals[:, c:c + 1],
               start=(c == 0), stop=(c == 1))
        cp(kslot, RV(PS[7].h[0:32, 304:305], PS[7].reg))
        for q in range(16):
            if q % 4 == 0:
                yield
            mm(PS[7][:, 308:309], RV(c_esel.h[:, q * 128:(q + 1) * 128], c_esel.reg), wslot)
            cp(s_all[:, q:q + 1], PS[7][:, 308:309])
            mm(PS[7][:, 312:313], RV(c_esel.h[:, q * 128:(q + 1) * 128], c_esel.reg), kslot)
            cp(idxf[:, q:q + 1], PS[7][:, 312:313])
        ts(idxg, idxf, 64.0, c_r64[:, 0:1], ALU.mult, ALU.add)
        cp(idxu, idxg)

    if dbg:
        dbgA = RV(nc.dram_tensor("dbgA", [128, 64], F32, kind="ExternalOutput").ap(), "d.dbgA")
        dbgB = RV(nc.dram_tensor("dbgB", [128, D], F32, kind="ExternalOutput").ap(), "d.dbgB")
        dbgC = RV(nc.dram_tensor("dbgC", [128, D], F32, kind="ExternalOutput").ap(), "d.dbgC")
        dbgD = RV(nc.dram_tensor("dbgD", [128, 64], F32, kind="ExternalOutput").ap(), "d.dbgD")
        dbgE = RV(nc.dram_tensor("dbgE", [128, S], F32, kind="ExternalOutput").ap(), "d.dbgE")
        dbgF = RV(nc.dram_tensor("dbgF", [128, D], BF16, kind="ExternalOutput").ap(), "d.dbgF")
        dbgG = RV(nc.dram_tensor("dbgG", [128, D], F32, kind="ExternalOutput").ap(), "d.dbgG")
        dbgH = RV(nc.dram_tensor("dbgH", [128, 64], F32, kind="ExternalOutput").ap(), "d.dbgH")

    run_gens(topk(c_l1r, c_l1c, s1_all, idx1))
    if dbg:
        dma("sync", dbgA[:, 0:16], s1_all)
        dma("sync", dbgA[:, 16:32], idxg)
        dma("sync", dbgA[:, 32:48], tk[:, 512:528])
        dma("sync", dbgA[:, 48:64], tk[:, 0:16])


    a1w = PB16.take(16 * 1024, "a1w")
    b1s = [PB16.take(16 * 1024, f"b1s{p}") for p in range(2)]
    w1st = [PB16.take(512, f"w1st{p}") for p in range(2)]
    a1stg = [PF32.take(1024, f"a1stg{p}") for p in range(2)]

    def a1w_loader():
        for q in range(16):
            sg = a1stg[q % 2]
            idma(sg, a1t_d, idx1[:, q:q + 1], 0)
            act(RV(a1w.h[:, q * 1024:(q + 1) * 1024], a1w.reg), sg, AF.Copy, scale=s1_all[:, q:q + 1])
            yield

    run_gens(topk(c_l2r, c_l2c, s2_all, idx2), a1w_loader())
    for nn in range(4):
        bb = b1s[nn % 2]
        for q in range(16):
            idma(RV(bb.h[:, q * 1024:(q + 1) * 1024], bb.reg), b1t_d, idx1[:, q:q + 1], nn * 1024)
        for half in range(2):
            n = nn * 2 + half
            for i in range(8):
                ps = PS[[0, 2, 1, 3][i % 4]]
                for q in range(16):
                    mm(ps, RV(a1w.h[:, q * 1024 + i * 128: q * 1024 + (i + 1) * 128], a1w.reg),
                       RV(bb.h[:, q * 1024 + half * 512: q * 1024 + (half + 1) * 512], bb.reg),
                       start=(q == 0), stop=(q == 15))
                wst = w1st[i % 2]
                act(wst, ps, AF.Copy, scale=1.0)
                dma("sync", w1c_d[:, i * FF + n * 512: i * FF + (n + 1) * 512], wst)
    b.barrier()
    PB16.reset()
    PF32.reset()

    b2w = PB16.take(16 * 1024, "b2w")
    a2b = [PB16.take(16 * 1024, f"a2b{p}") for p in range(2)]
    w2st = [PB16.take(512, f"w2st{p}") for p in range(2)]
    b2stg = [PF32.take(1024, f"b2stg{p}") for p in range(2)]
    for q in range(16):
        sg = b2stg[q % 2]
        idma(sg, b2t_d, idx2[:, q:q + 1], 0)
        act(RV(b2w.h[:, q * 1024:(q + 1) * 1024], b2w.reg), sg, AF.Copy, scale=s2_all[:, q:q + 1])
    for band in range(4):
        ab = a2b[band % 2]
        for q in range(16):
            idma(RV(ab.h[:, q * 1024:(q + 1) * 1024], ab.reg), a2t_d, idx2[:, q:q + 1], band * 1024)
        for i in range(8):
            for n in range(2):
                ps = PS[0] if (i * 2 + n) % 2 == 0 else PS[2]
                for q in range(16):
                    mm(ps, RV(ab.h[:, q * 1024 + i * 128: q * 1024 + (i + 1) * 128], ab.reg),
                       RV(b2w.h[:, q * 1024 + n * 512: q * 1024 + (n + 1) * 512], b2w.reg),
                       start=(q == 0), stop=(q == 15))
                wst = w2st[n]
                act(wst, ps, AF.Copy, scale=1.0)
                dma("sync", w2c_d[:, (band * 8 + i) * 1024 + n * 512: (band * 8 + i) * 1024 + (n + 1) * 512], wst)
    b.barrier()
    PB16.reset()
    PF32.reset()

    wv_b = PB16.take(8 * 1024, "wv")
    wo_b = PB16.take(8 * 1024, "wo")
    wfi_b = PB16.take(16, "wfi")
    hfm = [PB16.take(1024, f"hfm{p}") for p in range(3)]
    scb = [PB16.take(128, f"scb{p}") for p in range(3)]
    vtb = [PB16.take(1024, f"vt{p}") for p in range(3)]
    rbf = PB16.take(1040, "rbf")
    brow = PB16.take(128, "brow")
    abf = [PB16.take(16, f"abf{p}") for p in range(3)]

    bvr = PB16.take(1024, "bvr")
    bor = PB16.take(1024, "bor")
    xt = PF32.take(2 * D, "xt2")
    xin4 = PF32.take(5 * D, "xin5")
    hs1 = [PF32.take(D, f"hs1{p}") for p in range(3)]
    attn4 = PF32.take(4 * D, "attn4")
    zo4 = PF32.take(4 * D, "zo4")
    sg1p = PF32.take(D, "sg1p")
    sg2p = PF32.take(D, "sg2p")
    rst = PF32.take(1040, "rst")
    rstB2 = PF32.take(1040, "rstB2")
    rstB = rstB2
    scf1 = sg2p[:, 768:896]
    scf2 = sg2p[:, 896:1024]
    reps = sg1p[:, 896:1024]

    for c in range(8):
        dma("gpsimd", RV(wv_b.h[:, c * 1024:(c + 1) * 1024], wv_b.reg), wv_d[c * 128:(c + 1) * 128, :])
        dma("gpsimd", RV(wo_b.h[:, c * 1024:(c + 1) * 1024], wo_b.reg), wo_d[c * 128:(c + 1) * 128, :])
        dma("gpsimd", RV(wfi_b.h[:, c * 2:(c + 1) * 2], wfi_b.reg), wfi_d[c * 128:(c + 1) * 128, :])
    dma("sync", gc1[:, 0:8], ln1gc_d)
    dma("sync", bc1[:, 0:8], ln1bc_d)
    dma("gpsimd", RV(bvr.h[0:1, :], bvr.reg), bv_d)
    dma("gpsimd", RV(bor.h[0:1, :], bor.reg), bo_d)
    load_row_and_bcast(bfi_d, bfib, 2)
    mset(offs[:, 0:1], 0.0)
    mset(rst[0:1, 0:1040], 0.0)
    mset(RV(rbf.h[0:1, :], rbf.reg), 0.0)
    xb_rr = xb_d.h.rearrange("(tp p) f -> p tp f", p=128)

    def pa_early(t):
        par = t % 3
        sb0 = 32 * par
        xtt = xt[:, (t % 2) * D:(t % 2 + 1) * D]
        dma("sync", xtt, x_d[t * 128:(t + 1) * 128, :])
        xint = xin4[:, (t % 5) * D:(t % 5 + 1) * D]
        m0, r0 = ln_stats(xtt, sb0 + 0)
        yield
        ts(xint, xtt, m0, r0, ALU.subtract, ALU.mult)
        yield
        m1, r1 = ln_stats(xint, sb0 + 16)
        yield
        ts(hs1[par], xint, m1, r1, ALU.subtract, ALU.mult)

    def pa_late(t):
        par = t % 3
        p4 = t % 4
        fb0 = 16 * par
        hf = hfm[par]
        transpose_affine(hs1[par], lambda c, _h=hf: RV(_h.h[:, c * 128:(c + 1) * 128], _h.reg), gc1, bc1)
        yield
        for n in range(2):
            ps = PS[2 + n]
            for c in range(8):
                mm(ps, RV(hf.h[:, c * 128:(c + 1) * 128], hf.reg),
                   RV(wv_b.h[:, c * 1024 + n * 512: c * 1024 + (n + 1) * 512], wv_b.reg),
                   start=(c == 0), stop=False)
            mm(ps, RV(c_ones1b.h[0:1, :], c_ones1b.reg),
               RV(bvr.h[0:1, n * 512:(n + 1) * 512], bvr.reg), start=False, stop=True)
            cp(RV(vtb[par].h[:, n * 512:(n + 1) * 512], vtb[par].reg), ps)
        yield
        for n in range(2):
            ps = PS[2 + n]
            for c in range(8):
                mm(ps, RV(hf.h[:, c * 128:(c + 1) * 128], hf.reg),
                   RV(wo_b.h[:, c * 1024 + n * 512: c * 1024 + (n + 1) * 512], wo_b.reg),
                   start=(c == 0), stop=False)
            mm(ps, RV(c_ones1b.h[0:1, :], c_ones1b.reg),
               RV(bor.h[0:1, n * 512:(n + 1) * 512], bor.reg), start=False, stop=True)
            cp(zo4[:, p4 * D + n * 512: p4 * D + (n + 1) * 512], ps)
        yield
        for c in range(8):
            mm(PS[7][:, 0:2], RV(hf.h[:, c * 128:(c + 1) * 128], hf.reg),
               RV(wfi_b.h[:, c * 2:(c + 1) * 2], wfi_b.reg), start=(c == 0), stop=(c == 7))
        fz = fiz[:, fb0:fb0 + 16]
        cp(fz[:, 0:2], PS[7][:, 0:2])
        tt(fz[:, 2:4], fz[:, 0:2], bfib[:, 0:2], ALU.add)
        act(fz[:, 4:6], fz[:, 2:4], AF.Abs)
        act(fz[:, 6:8], fz[:, 4:6], AF.Exp, scale=-1.0)
        act(fz[:, 8:10], fz[:, 6:8], AF.Ln, bias=1.0)
        ts(fz[:, 10:12], fz[:, 2:4], 0.0, None, ALU.min)
        tt(fz[:, 12:14], fz[:, 10:12], fz[:, 8:10], ALU.subtract)
        yield
        mm(PS[7][:, 4:5], c_tri, fz[:, 12:13])
        mm(PS[7][:, 8:9], c_ones, fz[:, 12:13])
        p2 = t % 2
        tt(fz[:, 14:15], PS[7][:, 4:5], offs[:, p2:p2 + 1], ALU.add)
        tt(ccols[:, t:t + 1], fz[:, 13:14], fz[:, 14:15], ALU.subtract)
        tt(offs[:, 1 - p2:2 - p2], offs[:, p2:p2 + 1], PS[7][:, 8:9], ALU.add)
        ts(reps, c_ones[:, 0:128], fz[:, 14:15], None, ALU.mult)
        tr(PS[6][:, 0:128], reps)
        cp(dbc3[par], PS[6][:, 0:128])

    def pa_attention(t):
        par = t % 3
        p4 = t % 4
        dsl = dbc3[par]
        ts(scf1, dsl, ccols[:, t:t + 1], None, ALU.add)
        tt(scf2, scf1, c_maskm, ALU.mult)
        tt(scf1, scf2, c_maska, ALU.add)
        sc = scb[par]
        act(sc, scf1, AF.Exp)
        yield
        mm(PS[0], sc, RV(vtb[par].h[:, 0:512], vtb[par].reg), start=True, stop=(t == 0))
        mm(PS[1], sc, RV(vtb[par].h[:, 512:1024], vtb[par].reg), start=True, stop=(t == 0))
        mm(PS[4][:, 0:1], sc, c_ocbf, start=True, stop=(t == 0))
        if t > 0:
            ts(astt[0:1, 2:3], RV(dbc3[par].h[0:1, 0:1], dbc3[par].reg), -1.0, None, ALU.mult)
            act(RV(brow.h[0:1, 0:128], brow.reg), RV(dbc3[par].h[0:1, 0:128], dbc3[par].reg),
                AF.Exp, bias=RV(astt.h[0:1, 2:3], astt.reg))
            mm(PS[0], RV(brow.h[0:1, :], brow.reg), RV(rbf.h[0:1, 0:512], rbf.reg),
               start=False, stop=True)
            mm(PS[1], RV(brow.h[0:1, :], brow.reg), RV(rbf.h[0:1, 512:1024], rbf.reg),
               start=False, stop=True)
            mm(PS[4][:, 0:1], RV(brow.h[0:1, :], brow.reg), RV(rbf.h[0:1, 1024:1025], rbf.reg),
               start=False, stop=True)
        yield
        if t + 1 < NT:
            refn = RV(dbc3[(t + 1) % 3].h[:, 0:1], dbc3[(t + 1) % 3].reg)
            act(abf[par][:, 0:1], ccols[:, t:t + 1], AF.Exp, bias=refn)
            mm(RV(PS[5].h[0:1, 0:512], PS[5].reg), abf[par][:, 0:1], RV(vtb[par].h[:, 0:512], vtb[par].reg))
            mm(RV(PS[6].h[0:1, 0:512], PS[6].reg), abf[par][:, 0:1], RV(vtb[par].h[:, 512:1024], vtb[par].reg))
            mm(RV(PS[4].h[0:1, 300:301], PS[4].reg), abf[par][:, 0:1], c_ocbf)
            tt(astt[0:1, 3:4], RV(dbc3[(t + 1) % 3].h[0:1, 0:1], dbc3[(t + 1) % 3].reg),
               RV(dbc3[par].h[0:1, 0:1], dbc3[par].reg), ALU.subtract)
            act(astt[0:1, 4:5], astt[0:1, 3:4], AF.Exp)
            yield
            ts(rstB[0:1, 0:1025], rst[0:1, 0:1025], RV(astt.h[0:1, 4:5], astt.reg), None, ALU.mult)
            tt(rst[0:1, 0:512], rstB[0:1, 0:512], RV(PS[5].h[0:1, 0:512], PS[5].reg), ALU.add)
            tt(rst[0:1, 512:1024], rstB[0:1, 512:1024], RV(PS[6].h[0:1, 0:512], PS[6].reg), ALU.add)
            tt(rst[0:1, 1024:1025], rstB[0:1, 1024:1025], RV(PS[4].h[0:1, 300:301], PS[4].reg), ALU.add)
            act(RV(rbf.h[0:1, 0:1025], rbf.reg), rst[0:1, 0:1025], AF.Copy, scale=1.0)
        ts(astt[:, 0:1], PS[4][:, 0:1], 1e-6, None, ALU.add)
        recip(astt[:, 1:2], astt[:, 0:1])
        act(attn4[:, p4 * D:p4 * D + 512], PS[0], AF.Copy, scale=astt[:, 1:2])
        act(attn4[:, p4 * D + 512:(p4 + 1) * D], PS[1], AF.Copy, scale=astt[:, 1:2])

    def pa_att2(t):
        s0 = (t % 4) * D
        zp = zo4[:, s0:s0 + D]
        ap_ = attn4[:, s0:s0 + D]
        xp = xin4[:, (t % 5) * D:(t % 5 + 1) * D]
        act(sg1p, zp, AF.Abs)
        act(sg2p, sg1p, AF.Exp, scale=-1.0)
        act(sg1p, sg2p, AF.Ln, bias=1.0)
        ts(sg2p, zp, 0.0, None, ALU.min)
        tt(zp, sg2p, sg1p, ALU.subtract)
        act(sg1p, zp, AF.Exp)
        tt(sg2p, sg1p, ap_, ALU.mult, eng="gpsimd")
        tt(sg1p, sg2p, xp, ALU.add, eng="gpsimd")
        dma("gpsimd", xb_d[t * 128:(t + 1) * 128, :], sg1p)

    run_gens(pa_early(0))
    run_gens(pa_early(1))
    run_gens(pa_early(2))
    run_gens(pa_late(0))
    run_gens(pa_early(3))
    run_gens(pa_late(1))
    for t in range(NT):
        if t >= 1:
            pa_att2(t - 1)
        run_gens(pa_early(t + 4) if t + 4 < NT else None,
                 pa_late(t + 2) if t + 2 < NT else None,
                 pa_attention(t))
    pa_att2(NT - 1)
    b.barrier()
    PB16.reset()
    PF32.reset()

    GW = 512
    NG = S // GW
    w1a_b = PB16.take(8 * HA, "w1a")
    h2fm = [PB16.take(8 * GW, f"h2fm{p}") for p in range(2)]
    mst = [PB16.take(24 * GW, "mst")]

    gc2 = PF32.take(16, "gc2")
    bc2 = PF32.take(16, "bc2")
    b1cs = PF32.take(32, "b1c")
    xbg = [PF32.take(4 * D, f"xbg{p}") for p in range(2)]
    h2s = PF32.take(D, "h2s")
    zbb = [PF32.take(GW, f"zb{p}") for p in range(2)]
    lkb = [PF32.take(GW, f"lk{p}") for p in range(2)]
    rkb = [PF32.take(GW, f"rk{p}") for p in range(2)]

    for c in range(8):
        dma("gpsimd", RV(w1a_b.h[:, c * HA:(c + 1) * HA], w1a_b.reg), w1a_d[c * 128:(c + 1) * 128, :])
    dma("sync", gc2[:, 0:8], ln2gc_d)
    dma("sync", bc2[:, 0:8], ln2bc_d)
    dma("sync", b1cs[:, 0:24], b1c_d)

    xb_r = xb_d.h.rearrange("(tp p) f -> p tp f", p=128)

    def pb1_prologue(g):
        par = g % 2
        nb = xbg[par]
        dma("sync", RV(nb.h.rearrange("p (i f) -> p i f", i=4), nb.reg),
            RV(xb_r[:, g * 4:(g + 1) * 4, :], xb_d.reg))
        for i in range(4):
            xbt = xbg[par][:, i * 1024:(i + 1) * 1024]
            m2, r2 = ln_stats(xbt, 32 * par)
            yield
            ts(h2s, xbt, m2, r2, ALU.subtract, ALU.mult)
            yield
            hfmt = h2fm[par]
            transpose_affine(h2s, lambda c, _i=i, _h=hfmt: RV(_h.h[:, c * GW + _i * 128: c * GW + (_i + 1) * 128], _h.reg), gc2, bc2)
            yield

    def pb1_main(g):
        par = g % 2
        for fb in range(24):
            if fb % 2 == 0:
                yield
            ps = PS[[0, 2, 1, 3][fb % 4]]
            for c in range(8):
                mm(ps, RV(w1a_b.h[:, c * HA + fb * 128: c * HA + (fb + 1) * 128], w1a_b.reg),
                   RV(h2fm[par].h[:, c * GW:(c + 1) * GW], h2fm[par].reg), start=(c == 0), stop=(c == 7))
            sgn, zsq, fac = zbb[fb % 2], lkb[fb % 2], rkb[fb % 2]
            act(sgn, ps, AF.Sign, bias=b1cs[:, fb:fb + 1])
            act(zsq, ps, AF.Square, bias=b1cs[:, fb:fb + 1])
            ts(fac, sgn, 0.49995000000223516, 0.5000499999977648, ALU.mult, ALU.add)
            tt(RV(mst[0].h[:, fb * GW:(fb + 1) * GW], mst[0].reg), zsq, fac, ALU.mult)
            if fb == 11:
                dma("gpsimd", m_d[:, g * 24 * GW: g * 24 * GW + 12 * GW],
                    RV(mst[0].h[:, 0:12 * GW], mst[0].reg))
        dma("gpsimd", m_d[:, g * 24 * GW + 12 * GW:(g + 1) * 24 * GW],
            RV(mst[0].h[:, 12 * GW:24 * GW], mst[0].reg))

    run_gens(pb1_prologue(0))
    for g in range(NG):
        run_gens(pb1_prologue(g + 1) if g + 1 < NG else None, pb1_main(g))
    b.barrier()
    PB16.reset()
    PF32.reset()

    w2a_b = PB16.take(24 * 1024, "w2a")
    mgrp = [PB16.take(24 * GW, f"mgrp{p}") for p in range(2)]

    b2ab = PF32.take(D, "b2ab")
    xbg2 = [PF32.take(4 * D, f"xbg2{p}") for p in range(2)]
    xt2 = [PF32.take(D, f"xt2{p}") for p in range(2)]
    ybuf = [PF32.take(D, f"ybuf{p}") for p in range(2)]
    yb2 = PF32.take(D, "yb2")
    lns = PF32.take(D, "lns")
    x1st = [PF32.take(D, f"x1st{p}") for p in range(2)]

    for q in range(24):
        dma("gpsimd", RV(w2a_b.h[:, q * 1024:(q + 1) * 1024], w2a_b.reg), w2a_d[q * 128:(q + 1) * 128, :])
    load_row_and_bcast(b2a_d, b2ab, D)

    def pb2_loads(g):
        par = g % 2
        dma("sync", mgrp[par], m_d[:, g * 24 * GW:(g + 1) * 24 * GW])
        nb = xbg2[par]
        dma("sync", RV(nb.h.rearrange("p (i f) -> p i f", i=4), nb.reg),
            RV(xb_r[:, g * 4:(g + 1) * 4, :], xb_d.reg))

    def pb2_epi(g, i):
        par = g % 2
        tglob = g * 4 + i
        tp2 = tglob % 2
        yb = ybuf[i % 2]
        tt(yb2, yb, xbg2[par][:, i * 1024:(i + 1) * 1024], ALU.add)
        yield
        mo, ro = ln_stats(yb2, 32 * (i % 2))
        yield
        ts(lns, yb2, mo, ro, ALU.subtract, ALU.mult)
        yield
        tt(x1st[i % 2], lns, xt2[tp2], ALU.add)
        dma("gpsimd", x1_d[tglob * 128:(tglob + 1) * 128, :], x1st[i % 2])

    def pb2_mms(g, i):
        par = g % 2
        tglob = g * 4 + i
        tp2 = tglob % 2
        dma("gpsimd", xt2[tp2], x_d[tglob * 128:(tglob + 1) * 128, :])
        for n in range(2):
            ps = PS[1 + n] if i % 2 == 0 else PS[5 + n]
            for fb in range(24):
                if fb % 8 == 0:
                    yield
                mm(ps, RV(mgrp[par].h[:, fb * GW + i * 128: fb * GW + (i + 1) * 128], mgrp[par].reg),
                   RV(w2a_b.h[:, fb * 1024 + n * 512: fb * 1024 + (n + 1) * 512], w2a_b.reg),
                   start=(fb == 0), stop=(fb == 23))
            tt(ybuf[i % 2][:, n * 512:(n + 1) * 512], ps, b2ab[:, n * 512:(n + 1) * 512], ALU.add)

    pb2_loads(0)
    for g in range(NG):
        for i in range(4):
            other = pb2_epi(g, i - 1) if i >= 1 else (pb2_epi(g - 1, 3) if g >= 1 else None)
            run_gens(pb2_mms(g, i), other)
        if g + 1 < NG:
            pb2_loads(g + 1)
    run_gens(pb2_epi(NG - 1, 3))
    b.barrier()
    PB16.reset()
    PF32.reset()

    GW1 = 512
    NG1 = S // GW1
    w1c_b = PB16.take(8 * FF, "w1c")
    xnfm = [PB16.take(8 * GW1, f"xnfm{p}") for p in range(2)]
    gst = [PB16.take(4096, f"gst{p}") for p in range(2)]
    x1g = [PF32.take(4 * D, f"x1g{p}") for p in range(2)]
    xns = PF32.take(D, "xns")

    dma("sync", w1c_b, RV(w1c_d.h[:, :], w1c_d.reg))
    x1_r = x1_d.h.rearrange("(tp p) f -> p tp f", p=128)

    def pc1_prologue(g):
        par = g % 2
        nb = x1g[par]
        dma("sync", RV(nb.h.rearrange("p (i f) -> p i f", i=4), nb.reg),
            RV(x1_r[:, g * 4:(g + 1) * 4, :], x1_d.reg))
        for i in range(4):
            x1t = x1g[par][:, i * 1024:(i + 1) * 1024]
            mn, rn = ln_stats(x1t, 32 * par, via_pow=True)
            yield
            ts(xns, x1t, mn, rn, ALU.subtract, ALU.mult)
            yield
            xf = xnfm[par]
            transpose_tile(xns, lambda c, _i=i, _x=xf: RV(_x.h[:, c * GW1 + _i * 128: c * GW1 + (_i + 1) * 128], _x.reg))
            yield

    def pc1_main(g):
        par = g % 2
        for fb in range(32):
            if fb % 3 == 0:
                yield
            ps = PS[[0, 2, 1, 3][fb % 4]]
            for c in range(8):
                mm(ps, RV(w1c_b.h[:, c * FF + fb * 128: c * FF + (fb + 1) * 128], w1c_b.reg),
                   RV(xnfm[par].h[:, c * GW1:(c + 1) * GW1], xnfm[par].reg), start=(c == 0), stop=(c == 7))
            gs = gst[(fb // 8) % 2]
            act(RV(gs.h[:, (fb % 8) * 512:(fb % 8 + 1) * 512], gs.reg), ps, AF.Gelu_apprx_tanh)
            if fb % 8 == 7:
                dma("gpsimd", g_d[:, g * 16384 + (fb - 7) * 512: g * 16384 + (fb + 1) * 512], gs)

    run_gens(pc1_prologue(0))
    for g in range(NG1):
        run_gens(pc1_prologue(g + 1) if g + 1 < NG1 else None, pc1_main(g))
    b.barrier()
    PB16.reset()
    PF32.reset()

    w2c_b = PB16.take(32 * 1024, "w2c")
    gq = [PB16.take(4096, f"gq{q}") for q in range(4)]
    x1g2 = [PF32.take(4 * D, f"x1g2{p}") for p in range(2)]
    ybufg = [PF32.take(4 * D, f"ybufg{p}") for p in range(2)]
    lns2 = PF32.take(D, "lns2")
    outst = [PF32.take(D, f"outst{p}") for p in range(2)]

    dma("sync", w2c_b, RV(w2c_d.h[:, :], w2c_d.reg))
    for qd in range(4):
        dma("sync", gq[qd], RV(g_d.h[:, qd * 4096:(qd + 1) * 4096], g_d.reg))

    def pc2_epi(g):
        gp = g % 2
        for i in range(4):
            tglob = g * 4 + i
            yb = ybufg[gp][:, i * 1024:(i + 1) * 1024]
            mo, ro = ln_stats(yb, 32 * (i % 2), via_pow=True)
            yield
            ts(lns2, yb, mo, ro, ALU.subtract, ALU.mult)
            yield
            tt(outst[i % 2], lns2, x1g2[gp][:, i * 1024:(i + 1) * 1024], ALU.add)
            dma("gpsimd", out_d[tglob * 128:(tglob + 1) * 128, :], outst[i % 2])
            yield

    def pc2_mm(g):
        gp = g % 2
        dma("gpsimd", RV(x1g2[gp].h.rearrange("p (i f) -> p i f", i=4), x1g2[gp].reg),
            RV(x1_r[:, g * 4:(g + 1) * 4, :], x1_d.reg))
        for qd in range(4):
            if qd >= 1 and g + 1 < NG1:
                dma("sync", gq[qd - 1],
                    RV(g_d.h[:, (g + 1) * 16384 + (qd - 1) * 4096: (g + 1) * 16384 + qd * 4096], g_d.reg))
            for k in range(8):
                fb = qd * 8 + k
                if k % 3 == 0:
                    yield
                for i in range(4):
                    for n in range(2):
                        mm(PS[i * 2 + n],
                           RV(gq[qd].h[:, k * 512 + i * 128: k * 512 + (i + 1) * 128], gq[qd].reg),
                           RV(w2c_b.h[:, fb * 1024 + n * 512: fb * 1024 + (n + 1) * 512], w2c_b.reg),
                           start=(fb == 0), stop=(fb == 31))
        if g + 1 < NG1:
            dma("sync", gq[3],
                RV(g_d.h[:, (g + 1) * 16384 + 3 * 4096: (g + 2) * 16384], g_d.reg))
        for i in range(4):
            cp(ybufg[gp][:, i * 1024:i * 1024 + 512], PS[i * 2])
            cp(ybufg[gp][:, i * 1024 + 512:(i + 1) * 1024], PS[i * 2 + 1])

    for g in range(NG1):
        run_gens(pc2_mm(g), pc2_epi(g - 1) if g >= 1 else None)
    run_gens(pc2_epi(NG1 - 1))
    b.barrier()

    b.finalize()
    with nc.Block() as block:
        b.play(block, sems)
    st.close()
    return nc


def _host_common(inputs):
    g = lambda k: np.ascontiguousarray(np.asarray(inputs[k], dtype=np.float32))
    i = np.arange(128)
    esel = np.zeros((32, 2048), np.float32)
    for q in range(16):
        for p in range(128):
            esel[2 * q + p // 64, q * 128 + p] = 1.0
    wf = g("L0_attn_Wf")
    wi = g("L0_attn_Wi")
    return {
        "wv": g("L0_attn_Wv"),
        "wo": g("L0_attn_Wo"),
        "wfi": np.ascontiguousarray(np.concatenate([wf, wi], axis=1)),
        "w1a": g("L0_attn_W1"),
        "w2a": g("L0_attn_W2"),
        "bv": g("L0_attn_bv").reshape(1, D),
        "bo": g("L0_attn_bo").reshape(1, D),
        "bfi": np.ascontiguousarray(
            np.array([[np.asarray(inputs["L0_attn_bf"]).reshape(-1)[0],
                       np.asarray(inputs["L0_attn_bi"]).reshape(-1)[0]]], dtype=np.float32)),
        "g1": g("L0_attn_ln1_g").reshape(1, D),
        "b1": g("L0_attn_ln1_b").reshape(1, D),
        "g2": g("L0_attn_ln2_g").reshape(1, D),
        "b2": g("L0_attn_ln2_b").reshape(1, D),
        "b1c": np.ascontiguousarray(g("L0_attn_b1").reshape(24, 128).T),
        "ln1gc": np.ascontiguousarray(g("L0_attn_ln1_g").reshape(8, 128).T),
        "ln1bc": np.ascontiguousarray(g("L0_attn_ln1_b").reshape(8, 128).T),
        "ln2gc": np.ascontiguousarray(g("L0_attn_ln2_g").reshape(8, 128).T),
        "ln2bc": np.ascontiguousarray(g("L0_attn_ln2_b").reshape(8, 128).T),
        "b2a": g("L0_attn_b2").reshape(1, D),
        "a1t": np.ascontiguousarray(np.asarray(inputs["L0_ffn_fc1_A"], np.float32).transpose(0, 2, 1).reshape(NEXP * RK, D)),
        "b1t": np.ascontiguousarray(np.asarray(inputs["L0_ffn_fc1_B"], np.float32).reshape(NEXP * RK, FF)),
        "a2t": np.ascontiguousarray(np.asarray(inputs["L0_ffn_fc2_A"], np.float32).transpose(0, 2, 1).reshape(NEXP * RK, FF)),
        "b2t": np.ascontiguousarray(np.asarray(inputs["L0_ffn_fc2_B"], np.float32).reshape(NEXP * RK, D)),
        "l1r": g("L0_ffn_fc1_logits").reshape(1, NEXP),
        "l1c": np.ascontiguousarray(g("L0_ffn_fc1_logits").reshape(2, 128).T),
        "l2r": g("L0_ffn_fc2_logits").reshape(1, NEXP),
        "l2c": np.ascontiguousarray(g("L0_ffn_fc2_logits").reshape(2, 128).T),
        "tri": np.ascontiguousarray(np.triu(np.ones((128, 128), np.float32))),
        "onesq": np.ones((128, 128), np.float32),
        "ones1": np.ones((1, 128), np.float32),
        "idf": np.ascontiguousarray(np.eye(128, dtype=np.float32)),
        "maskm": np.ascontiguousarray((i[:, None] <= i[None, :]).astype(np.float32)),
        "maska": np.ascontiguousarray(np.where(i[:, None] <= i[None, :], 0.0, -100.0).astype(np.float32)),
        "iota32": np.ascontiguousarray(np.broadcast_to(np.arange(32, dtype=np.float32)[None, :], (128, 32)).copy()),
        "kvals": np.ascontiguousarray(np.stack([i, i + 128], axis=1).astype(np.float32)),
        "r64": (i % 64).astype(np.float32).reshape(128, 1),
        "esel": esel,
        "ocbf": np.ones((128, 1), ml_dtypes.bfloat16),
        "ones1b": np.ones((1, 128), ml_dtypes.bfloat16),
    }


def kernel(**inputs):
    x = np.asarray(inputs["L0_attn_x"], dtype=np.float32)
    common = _host_common(inputs)
    nc = build_program()
    in_maps = [dict(common, x=np.ascontiguousarray(x[c])) for c in range(8)]
    res = run_bass_kernel_spmd(nc, in_maps, core_ids=list(range(8)))
    return np.stack([np.asarray(res.results[c]["out"]) for c in range(8)], axis=0)
